# version 18
# baseline (speedup 1.0000x reference)
from contextlib import ExitStack

import numpy as np
import ml_dtypes

import concourse.bass as bass
import concourse.mybir as mybir
from concourse.bass_utils import run_bass_kernel_spmd
from concourse.alu_op_type import AluOpType as OP

F32 = mybir.dt.float32
BF16 = mybir.dt.bfloat16
AF = mybir.ActivationFunctionType
BFNP = ml_dtypes.bfloat16

D = 2048
NH = 32
DFF = 8192
T = 1024
HALO = 3
TN = T + HALO
Q = 128
NCH = 8
EPS = 1e-5
CLAMP = 30.0

_DEBUG = []
PHASE_LIMIT = 99
SCAN_PART = 99

ENGS = ("sync", "gpsimd", "vector", "scalar", "tensor")
NTS = ((0, 512), (512, 512))
NTS_H = ((0, 512), (512, 512), (1024, 3))


LAD_KS = (128, 64, 32, 16, 8, 4, 2, 1)
LAD_ACT = (1, 3, 5, 6, 7)


def _lad_thr():
    d = np.float32(0.0)
    thr = []
    for j, k in enumerate(LAD_KS):
        thr.append(np.float32(np.float32(k) - np.float32(128.5) + d))
        if j in LAD_ACT:
            d = np.float32(d + np.float32(k) * np.float32(0.5))
    return thr, d


LAD_THR, LAD_D = _lad_thr()
LAD_FC = float(np.float32(128.0) - LAD_D)


def _hadamard(n):
    H = np.array([[1.0]], dtype=np.float64)
    while H.shape[0] < n:
        H = np.block([[H, H], [H, -H]])
    return H.astype(np.float32)


class S:

    def __init__(self, nc, stk):
        self.nc = nc
        self.stk = stk
        self.q = {e: [] for e in ENGS}
        self.sems = {}
        self.cnt = {}

    def sem(self, name):
        if name not in self.sems:
            self.sems[name] = self.stk.enter_context(self.nc.semaphore(name))
            self.cnt[name] = 0
        return self.sems[name]

    def emit(self, eng, fn, inc=None, amt=1):
        dr = eng in ("vector", "scalar", "gpsimd")
        if inc is not None:
            self.sem(inc)
            self.cnt[inc] += amt
            h = self.sems[inc]
            if dr:
                def g(e, f=fn, hh=h, a=amt):
                    f(e).then_inc(hh, a)
                    e.drain()
                self.q[eng].append(g)
            else:
                self.q[eng].append(
                    lambda e, f=fn, hh=h, a=amt: f(e).then_inc(hh, a))
        else:
            if dr:
                def g(e, f=fn):
                    f(e)
                    e.drain()
                self.q[eng].append(g)
            else:
                self.q[eng].append(fn)
        return self.cnt.get(inc)

    def dma(self, eng, out, in_, inc):
        return self.emit(eng, lambda e, o=out, i=in_: e.dma_start(out=o, in_=i),
                         inc=inc, amt=16)

    def now(self, sem):
        return self.cnt.get(sem, 0)

    def wait(self, eng, sem, val=None):
        self.sem(sem)
        v = self.cnt[sem] if val is None else val
        h = self.sems[sem]
        self.q[eng].append(lambda e, hh=h, vv=v: e.wait_ge(hh, vv))

    def barrier(self, engines=ENGS):
        snap = [(n, self.cnt[n]) for n in self.sems if self.cnt[n] > 0]
        for eng in engines:
            for n, v in snap:
                self.wait(eng, n, v)

    def build(self, block):
        for ename in ENGS:
            ops = self.q[ename]
            dec = getattr(block, ename)

            def _body(eng, _ops=ops):
                for fn in _ops:
                    fn(eng)
            dec(_body)


class Stream2:

    def __init__(self, P, eng, tag, nslots=2):
        self.P, self.eng = P, eng
        self.names = [f"{tag}{i}" for i in range(nslots)]
        for n in self.names:
            P.S.sem(n)
        self.tok = [P.S.now(n) for n in self.names]

    def load(self, slot, out, in_, guard=None):
        if guard is not None:
            gl = guard if isinstance(guard, list) else [guard]
            for g in gl:
                self.P.S.wait(self.eng, *g)
        self.P.S.dma(self.eng, out, in_, self.names[slot])
        self.tok[slot] = self.P.S.now(self.names[slot])

    def wait(self, eng, slot):
        self.P.S.wait(eng, self.names[slot], self.tok[slot])


class Arena:

    def __init__(self, nc, stk):
        self.t = None
        for trial, kb in enumerate(range(220, 120, -2)):
            try:
                self.t = stk.enter_context(
                    nc.sbuf_tensor(f"arena{trial}", [128, kb * 512], BF16))
                self.size = kb * 1024
                break
            except AssertionError:
                continue
        assert self.t is not None, "arena allocation failed"
        self.free_list = [(0, self.size)]

    def alloc(self, nbytes, hi=False):
        nbytes = (nbytes + 63) & ~63
        idxs = range(len(self.free_list) - 1, -1, -1) if hi \
            else range(len(self.free_list))
        for i in idxs:
            o, s = self.free_list[i]
            if s >= nbytes:
                if s == nbytes:
                    self.free_list.pop(i)
                    return o
                if hi:
                    self.free_list[i] = (o, s - nbytes)
                    return o + s - nbytes
                self.free_list[i] = (o + nbytes, s - nbytes)
                return o
        raise AssertionError(
            f"arena OOM: need {nbytes}, free={self.free_list}")

    def free(self, off, nbytes):
        nbytes = (nbytes + 63) & ~63
        self.free_list.append((off, nbytes))
        self.free_list.sort()
        merged = []
        for o, s in self.free_list:
            if merged and merged[-1][0] + merged[-1][1] == o:
                merged[-1] = (merged[-1][0], merged[-1][1] + s)
            else:
                assert not merged or merged[-1][0] + merged[-1][1] <= o, \
                    f"arena double-free/overlap at {off}: {self.free_list}"
                merged.append([o, s])
        self.free_list = [tuple(x) for x in merged]


class Prog:
    def __init__(self):
        self.nc = bass.Bass("TRN2", target_bir_lowering=False)
        self.stk = ExitStack()
        self.S = S(self.nc, self.stk)
        self.hbm = {}
        self._declare_hbm()
        E = self.stk.enter_context
        self.pb = [E(self.nc.psum_tensor(f"pb{i}", [128, 512], F32))
                   for i in range(7)]
        self.pbT = E(self.nc.psum_tensor("pbT", [128, 1024], BF16))
        self._alloc_const()
        self.arena = Arena(self.nc, self.stk)
        self._uid = 0

    def uid(self):
        self._uid += 1
        return self._uid

    def sb(self, stk, shape, dt, name=None, hi=False):
        p, n = shape
        esz = 2 if dt == BF16 else 4
        nbytes = n * esz
        off = self.arena.alloc(nbytes, hi=hi)
        if stk is not None:
            stk.callback(self.arena.free, off, nbytes)
        view = self.t_arena()[0:p, off // 2:off // 2 + (nbytes + 1) // 2]
        return view if dt == BF16 else view.bitcast(F32)

    def t_arena(self):
        return self.arena.t

    def _declare_hbm(self):
        nc = self.nc

        def add(n, shp, dt=F32, kind="ExternalInput"):
            self.hbm[n] = nc.dram_tensor(n, shp, dt, kind=kind).ap()
        add("xT", [D, TN])
        add("xprevT", [D, TN])
        add("winT", [41, 128, 2048])
        add("woutT", [16, 128, 2048])
        add("wgT", [64, 128, 2048])
        add("wuT", [64, 128, 2048])
        add("wdT", [16, 128, 8192])
        add("fgT", [64, 128, 2048])
        add("fuT", [64, 128, 2048])
        add("fdT", [16, 128, 8192])
        add("convw", [128, 96])
        add("convb", [128, 24])
        add("dtb", [32, 1])
        add("alog", [32, 1])
        add("wln", [128, 16])
        add("wpost", [128, 16])
        add("ssmw", [128, 16])
        add("ddiag", [128, 2048])
        add("h128pm", [128, 256])
        add("h64d", [128, 128])
        add("tri4", [128, 512])
        add("identf", [128, 128])
        add("rowc", [1, 448])
        add("ladthr", [128, 24])
        add("onescol", [128, 1])
        add("scA", [DFF, T], BF16, "Internal")
        add("sch2", [D, T], F32, "Internal")
        add("outT", [D, T], F32, "ExternalOutput")
        for n, shp, dt in _DEBUG:
            add(n, shp, dt, "ExternalOutput")

    def dump(self, name, ap, eng="sync"):
        if name in self.hbm:
            self.S.barrier()
            self.S.dma(eng, self.hbm[name], ap, "st")
            self.S.barrier()

    def _alloc_const(self):
        nc, Ss, E = self.nc, self.S, self.stk.enter_context
        self.identf = E(nc.sbuf_tensor("c_identf", [128, 128], F32))
        self.identb = E(nc.sbuf_tensor("c_identb", [128, 128], BF16))
        self.h128pm = E(nc.sbuf_tensor("c_h128pm", [128, 256], BF16))
        self.h64d = E(nc.sbuf_tensor("c_h64d", [128, 128], BF16))
        self.tri4 = E(nc.sbuf_tensor("c_tri4", [128, 512], BF16))
        self.convw = E(nc.sbuf_tensor("c_convw", [128, 96], F32))
        self.convb = E(nc.sbuf_tensor("c_convb", [128, 24], F32))
        self.dtb = E(nc.sbuf_tensor("c_dtb", [32, 1], F32))
        self.alog = E(nc.sbuf_tensor("c_alog", [32, 1], F32))
        self.aexp = E(nc.sbuf_tensor("c_aexp", [32, 1], F32))
        self.wln = E(nc.sbuf_tensor("c_wln", [128, 16], F32))
        self.wpost = E(nc.sbuf_tensor("c_wpost", [128, 16], F32))
        self.ssmw = E(nc.sbuf_tensor("c_ssmw", [128, 16], F32))
        self.ddiag = E(nc.sbuf_tensor("c_ddiag", [128, 2048], BF16))
        self.rowc = E(nc.sbuf_tensor("c_rowc", [1, 448], F32))
        self.ladthr = E(nc.sbuf_tensor("c_ladthr", [128, 24], F32))
        self.onescol = E(nc.sbuf_tensor("c_onescol", [128, 1], F32))
        self.zeros32 = E(nc.sbuf_tensor("c_zeros32", [32, Q], F32))
        h = self.hbm
        for dst, src in [(self.identf, h["identf"]), (self.convw, h["convw"]),
                         (self.convb, h["convb"]), (self.dtb, h["dtb"]),
                         (self.alog, h["alog"]), (self.wln, h["wln"]),
                         (self.wpost, h["wpost"]), (self.ssmw, h["ssmw"]),
                         (self.rowc, h["rowc"]), (self.onescol, h["onescol"]),
                         (self.ladthr, h["ladthr"])]:
            Ss.dma("sync", dst[:], src, "win")
        for dst, src in [(self.identb, h["identf"]), (self.h128pm, h["h128pm"]),
                         (self.h64d, h["h64d"]), (self.tri4, h["tri4"]),
                         (self.ddiag, h["ddiag"])]:
            Ss.dma("gpsimd", dst[:], src, "gpd")
        Ss.emit("vector", lambda e: e.memset(self.zeros32[:], 0.0))
        Ss.wait("scalar", "win", None)
        Ss.emit("scalar", lambda e: e.activation(self.aexp[:], self.alog[:], AF.Exp),
                inc="a")
        self.ones128 = self.rowc[0:1, 0:128]
        self.top64 = self.rowc[0:1, 128:256]
        self.bot64 = self.rowc[0:1, 256:384]
        self.c_invrn1 = self.rowc[0:1, 384:385]
        self.c_invrn2 = self.rowc[0:1, 385:386]
        self.c_nel = self.rowc[0:1, 386:387]

    def bcast_row(self, row_ap, dst_ap, psum, n):
        Ss = self.S
        Ss.wait("tensor", "v", Ss.now("v"))
        Ss.wait("tensor", "a", Ss.now("a"))
        last = None
        for c0 in range(0, n, 512):
            cs = min(512, n - c0)
            if last is not None:
                Ss.wait("tensor", *last)
            Ss.emit("tensor", lambda e, a=row_ap, d=psum, c=c0, s=cs:
                    e.matmul(d[:, 0:s], self.ones128, a[0:1, c:c + s],
                             start=True, stop=True), inc="t")
            tc = Ss.now("t")
            Ss.wait("vector", "t", tc)
            cnt = Ss.emit("vector", lambda e, d=dst_ap, p=psum, c=c0, s=cs:
                          e.tensor_copy(d[:, c:c + s], p[:, 0:s]), inc="v")
            last = ("v", cnt)
        return last

    def stream_mm(self, wname, m_list, rhs_fn, evac_fn, ps_pair, wbufs, kb,
                  kcol0=0, nts=NTS, mrows_fn=None):
        Ss = self.S
        wap = self.hbm[wname]
        st2 = Stream2(self, "gpsimd", f"wm{self.uid()}_")
        buf_use = [0, 0]

        def load(mi):
            m = m_list[mi]
            st2.load(mi % 2, wbufs[mi % 2][:, 0:kb * 128],
                     wap[m, :, kcol0 * 128:(kcol0 + kb) * 128],
                     guard=(("t", buf_use[mi % 2]) if buf_use[mi % 2] else None))
        load(0)
        if len(m_list) > 1:
            load(1)
        ps_rel = [None, None]
        it = 0
        for mi, m in enumerate(m_list):
            mr = mrows_fn(m) if mrows_fn else 128
            for ni, (n0, nsz) in enumerate(nts):
                pi = it % 2
                ps = ps_pair[pi]
                if ps_rel[pi] is not None:
                    Ss.wait("tensor", *ps_rel[pi])
                if ni == 0:
                    st2.wait("tensor", mi % 2)
                for j in range(kb):
                    Ss.emit("tensor",
                            lambda e, pj=ps, w=wbufs[mi % 2], jj=j, nn0=n0,
                            nns=nsz, st=(j == 0), sp=(j == kb - 1), mm=mr:
                            e.matmul(pj[0:mm, 0:nns],
                                     w[:, jj * 128:jj * 128 + mm],
                                     rhs_fn(jj, nn0, nns),
                                     start=st, stop=sp),
                            inc=("t" if j == kb - 1 else None))
                tdone = Ss.now("t")
                buf_use[mi % 2] = tdone
                ps_rel[pi] = evac_fn(m, ni, n0, nsz, ps, tdone)
                it += 1
            if mi + 2 < len(m_list):
                load(mi + 2)

    def ln_stats(self, src_hbm, ncols, stat, stk, with_rho):
        Ss = self.S
        xb = [self.sb(stk, [128, ncols], F32) for _ in range(2)]
        sq = self.sb(stk, [128, ncols], F32)
        scr = self.sb(stk, [1, 2 * ncols], F32)
        epsc = self.sb(stk, [1, 1], F32)
        Ss.emit("vector", lambda e: e.memset(epsc[:], EPS), inc="v")
        segs = [(0, 512), (512, 512)]
        if ncols > 1024:
            segs.append((1024, ncols - 1024))
        ps_s = [self.pb[1], self.pb[2], self.pb[3]]
        ps_q = [self.pb[4], self.pb[5], self.pb[6]]
        st2 = Stream2(self, "sync", f"ln{self.uid()}_")
        st2.load(0, xb[0][:], src_hbm[0:128, 0:ncols])
        st2.load(1, xb[1][:], src_hbm[128:256, 0:ncols])
        tprev = 0
        for c in range(16):
            st2.wait("scalar", c % 2)
            if c >= 1:
                Ss.wait("scalar", "t", tprev)
            acnt = Ss.emit("scalar", lambda e, b=xb[c % 2]:
                           e.activation(sq[:], b[:], AF.Square), inc="a")
            Ss.wait("tensor", "a", acnt)
            st2.wait("tensor", c % 2)
            for si, (s0, sz) in enumerate(segs):
                for psm, srcb in ((ps_s[si], xb[c % 2]), (ps_q[si], sq)):
                    Ss.emit("tensor", lambda e, p=psm, b=srcb, s0_=s0, sz_=sz,
                            st=(c == 0), sp=(c == 15):
                            e.matmul(p[0:1, 0:sz_], self.onescol[:, 0:1],
                                     b[:, s0_:s0_ + sz_], start=st, stop=sp),
                            inc="t")
            tprev = Ss.now("t")
            if c + 2 < 16:
                st2.load(c % 2, xb[c % 2][:],
                         src_hbm[(c + 2) * 128:(c + 3) * 128, 0:ncols],
                         guard=[("t", tprev), ("a", acnt)])
        Ss.wait("vector", "t", tprev)
        for si, (s0, sz) in enumerate(segs):
            Ss.emit("vector", lambda e, p=ps_s[si], s0_=s0, sz_=sz:
                    e.tensor_copy(scr[0:1, s0_:s0_ + sz_], p[0:1, 0:sz_]))
            Ss.emit("vector", lambda e, p=ps_q[si], s0_=s0, sz_=sz:
                    e.tensor_copy(scr[0:1, ncols + s0_:ncols + s0_ + sz_],
                                  p[0:1, 0:sz_]), inc="v")
        N = ncols
        mean = stat[0:1, 0:N]
        rr_ = stat[0:1, N:2 * N]
        rrho = stat[0:1, 2 * N:3 * N]
        s1 = scr[0:1, 0:N]
        s2 = scr[0:1, N:2 * N]
        em, wt = Ss.emit, Ss.wait
        inv_d = 1.0 / 2048.0
        em("vector", lambda e: e.tensor_single_scalar(mean, s1, inv_d, OP.mult))
        em("vector", lambda e: e.tensor_single_scalar(s2, s2, inv_d, OP.mult))
        em("vector", lambda e: e.tensor_tensor(s1, mean, mean, OP.mult))
        v = em("vector", lambda e: e.tensor_tensor(s2, s2, s1, OP.subtract),
               inc="v")
        wt("scalar", "v", v)
        a = em("scalar", lambda e: e.activation(s1, s2, AF.Sqrt,
                                                bias=epsc[0:1, 0:1]), inc="a")
        wt("vector", "a", a)
        v = em("vector", lambda e: e.reciprocal(rr_, s1), inc="v")
        if with_rho:
            em("vector", lambda e: e.tensor_tensor(s1, rr_, rr_, OP.mult))
            v = em("vector", lambda e: e.tensor_tensor(s1, s1, s2, OP.mult),
                   inc="v")
            wt("scalar", "v", v)
            a = em("scalar", lambda e: e.activation(s1, s1, AF.Sqrt,
                                                    bias=epsc[0:1, 0:1]), inc="a")
            wt("vector", "a", a)
            em("vector", lambda e: e.reciprocal(s1, s1) if False else
               e.reciprocal(s2, s1))
            v = em("vector", lambda e: e.tensor_tensor(rrho, rr_, s2, OP.mult),
                   inc="v")
        return ("v", v)

    def prescale(self, src_hbm, ncols, mbc, rrbc, wcol, dst, dst_stride, stk,
                 tag):
        Ss = self.S
        xb = [self.sb(stk, [128, ncols], F32) for _ in range(2)]
        tmp = self.sb(stk, [128, ncols], F32)
        st2 = Stream2(self, "sync", f"pr{self.uid()}_")
        st2.load(0, xb[0][:], src_hbm[0:128, 0:ncols])
        st2.load(1, xb[1][:], src_hbm[128:256, 0:ncols])
        aprev = 0
        vprev = 0
        for c in range(16):
            st2.wait("vector", c % 2)
            if wcol is not None and c >= 1:
                Ss.wait("vector", "a", aprev)
            Ss.emit("vector", lambda e, b=xb[c % 2]:
                    e.tensor_tensor(tmp[:], b[:], mbc[:, 0:ncols], OP.subtract))
            if wcol is None:
                vcnt = Ss.emit("vector", lambda e, cc=c:
                               e.tensor_tensor(dst[:, cc * dst_stride:cc * dst_stride + ncols],
                                               tmp[:], rrbc[:, 0:ncols], OP.mult),
                               inc="v")
            else:
                vcnt = Ss.emit("vector", lambda e:
                               e.tensor_tensor(tmp[:], tmp[:], rrbc[:, 0:ncols], OP.mult),
                               inc="v")
                Ss.wait("scalar", "v", vcnt)
                aprev = Ss.emit("scalar", lambda e, cc=c:
                        e.activation(dst[:, cc * dst_stride:cc * dst_stride + ncols],
                                     tmp[:], AF.Identity, scale=wcol[:, cc:cc + 1]),
                        inc="a")
            vprev = Ss.now("v")
            if c + 2 < 16:
                st2.load(c % 2, xb[c % 2][:],
                         src_hbm[(c + 2) * 128:(c + 3) * 128, 0:ncols],
                         guard=("v", vprev))

    def crop_store(self, n0, nsz):
        if n0 == 0:
            return 3, 0, nsz - 3
        return 0, n0 - 3, nsz

    def conv_chunk(self, c, rb, stk):
        Ss = self.S
        Ss.wait("scalar", "v", Ss.now("v"))
        toks = []
        for hi, h0 in enumerate((0, 512)):
            psm = self.accq[:, hi * 512:(hi + 1) * 512]
            acnt = Ss.emit("scalar", lambda e, p=psm, b=rb, hh=h0, cc=c:
                           e.activation(p, b[:, hh:hh + 512],
                                        AF.Identity,
                                        scale=self.convw[:, cc * 4:cc * 4 + 1],
                                        bias=self.convb[:, cc:cc + 1]), inc="a")
            Ss.wait("vector", "a", acnt)
            vcnt = None
            for k in (1, 2, 3):
                vcnt = Ss.emit("vector", lambda e, p=psm, b=rb, hh=h0, cc=c, kk=k:
                               e.scalar_tensor_tensor(
                                   p, b[:, hh + kk:hh + kk + 512],
                                   self.convw[:, cc * 4 + kk:cc * 4 + kk + 1],
                                   p, OP.mult, OP.add), inc="v")
            Ss.wait("scalar", "v", vcnt)
            acnt = Ss.emit("scalar", lambda e, p=psm, hh=h0, cc=c:
                           e.activation(self.xbc[:, cc * T + hh:cc * T + hh + 512],
                                        p, AF.Silu), inc="a")
            toks.append(("a", acnt))
        return toks[-1]

    def dt_rows(self, stk):
        Ss = self.S
        rA = lambda i: self.rowsA[0:32, i * T:(i + 1) * T]
        rB = lambda i: self.rowsB[0:32, i * T:(i + 1) * T]
        dtraw, dtv, L, expL = rB(0), rB(1), rA(0), rA(1)
        wgt, ldtmL, scr, scr2 = rA(2), rA(3), rB(2), rB(3)
        rsl = lambda i, a, b: (self.rowsA if i < 4 else None, )
        em, wt = Ss.emit, Ss.wait
        em("vector", lambda e: e.tensor_scalar(scr, dtraw, self.dtb[:, 0:1],
                                               None, OP.add))
        v = em("vector", lambda e: e.tensor_copy(scr2, scr), inc="v")
        wt("scalar", "v", v)
        a = em("scalar", lambda e: e.activation(scr2, scr2, AF.Abs), inc="a")
        a = em("scalar", lambda e: e.activation(scr2, scr2, AF.Exp, scale=-1.0),
               inc="a")
        a = em("scalar", lambda e: e.activation(scr2, scr2, AF.Ln, bias=1.0),
               inc="a")
        wt("vector", "a", a)
        em("vector", lambda e: e.tensor_single_scalar(scr, scr, 0.0, OP.max))
        v = em("vector", lambda e: e.tensor_tensor(dtv, scr, scr2, OP.add),
               inc="v")
        em("vector", lambda e: e.tensor_scalar(scr, dtv, self.aexp[:, 0:1],
                                               -1.0, OP.mult, OP.mult))
        for i in range(NCH):
            v = em("vector", lambda e, ii=i: e.tensor_tensor_scan(
                self.rowsA[0:32, ii * Q:(ii + 1) * Q],
                self.rowsB[0:32, 2 * T + ii * Q:2 * T + (ii + 1) * Q],
                self.zeros32[:], 0.0, OP.add, OP.add), inc="v")
        wt("scalar", "v", v)
        a = em("scalar", lambda e: e.activation(expL, L, AF.Exp), inc="a")
        v = em("vector", lambda e: e.tensor_single_scalar(scr2, dtv, 1e-38,
                                                          OP.max), inc="v")
        wt("scalar", "v", v)
        a = em("scalar", lambda e: e.activation(scr2, scr2, AF.Ln), inc="a")
        wt("vector", "a", a)
        em("vector", lambda e: e.tensor_tensor(ldtmL, scr2, L, OP.subtract))
        for i in range(NCH):
            v = em("vector", lambda e, ii=i: e.tensor_scalar(
                self.rowsB[0:32, 2 * T + ii * Q:2 * T + (ii + 1) * Q],
                self.rowsA[0:32, ii * Q:(ii + 1) * Q],
                self.rowsA[0:32, ii * Q + Q - 1:ii * Q + Q],
                -1.0, OP.subtract, OP.mult), inc="v")
        wt("scalar", "v", v)
        a = em("scalar", lambda e: e.activation(scr2, scr, AF.Exp), inc="a")
        wt("vector", "a", a)
        v = em("vector", lambda e: e.tensor_tensor(wgt, dtv, scr2, OP.mult),
               inc="v")
        return ("v", v)

    def stage1(self, src_name, stk_outer, full, stat):
        Ss = self.S
        src = self.hbm[src_name]
        stk = ExitStack()
        with ExitStack() as lstk:
            self.ln_stats(src, TN, stat, lstk, with_rho=True)
        Ss.barrier()
        rm1 = self.rm1
        with ExitStack() as lstk:
            mbc = self.sb(lstk, [128, TN], F32)
            rrbc = self.sb(lstk, [128, TN], F32)
            Ss.barrier(("tensor", "vector"))
            self.bcast_row(stat[0:1, 0:TN], mbc, self.pb[0], TN)
            self.bcast_row(stat[0:1, 2 * TN:3 * TN], rrbc, self.pb[0], TN)
            self.prescale(src, TN, mbc, rrbc, self.wln, rm1, TN, lstk, "rm1")
            Ss.barrier()
        wbufs = [self.sb(stk, [128, 2048], BF16) for _ in range(2)]
        ring = [self.sb(stk, [128, TN], BF16) for _ in range(2)]
        self.accq = self.sb(stk, [128, 1024], F32)
        m_list = list(range(41)) if full else [16, 17, 18, 19, 20, 21, 22, 23, 40]
        conv_tok = {}
        evac_state = {"blk": {}}

        def rhs_fn(k, n0, nsz):
            return rm1[:, k * TN + n0:k * TN + n0 + nsz]

        def evac(m, ni, n0, nsz, ps, tdone):
            po, do_, dsz = self.crop_store(n0, nsz)
            if m < 16:
                Ss.wait("scalar", "t", tdone)
                cnt = Ss.emit("scalar", lambda e, p=ps, mm=m:
                              e.activation(self.siluz[:, mm * T + do_:mm * T + do_ + dsz],
                                           p[:, po:po + dsz], AF.Silu), inc="a")
                return ("a", cnt)
            if m == 40:
                Ss.wait("vector", "t", tdone)
                cnt = Ss.emit("vector", lambda e, p=ps:
                              e.tensor_copy(self.rowsB[0:32, do_:do_ + dsz],
                                            p[0:32, po:po + dsz]), inc="v")
                return ("v", cnt)
            c = m - 16
            rb = ring[c % 2]
            if c >= 2 and ni == 0 and (c - 2) in conv_tok:
                Ss.wait("vector", *conv_tok[c - 2])
            Ss.wait("vector", "t", tdone)
            cnt = Ss.emit("vector", lambda e, p=ps, r=rb, nn0=n0, nns=nsz:
                          e.tensor_copy(r[:, nn0:nn0 + nns], p[:, 0:nns]),
                          inc="v")
            if ni == len(NTS_H) - 1:
                conv_tok[c] = self.conv_chunk(c, rb, stk)
            return ("v", cnt)

        self.stream_mm("winT", m_list, rhs_fn, evac, (self.pb[0], self.pb[1]),
                       wbufs, 16, nts=NTS_H)
        Ss.barrier()
        with ExitStack() as lstk:
            self.dt_rows(lstk)
        Ss.barrier()
        stk.close()

    def scan(self, stk, state_only):
        Ss = self.S
        Bz = self.sb(stk, [128, 16 * Q], BF16)
        Xt = [self.sb(stk, [128, 128], BF16) for _ in range(4)]
        Bt = [self.sb(stk, [128, 128], BF16) for _ in range(4)]
        wld = self.sb(stk, [128, 64], F32)
        if not state_only:
            lflat = self.sb(stk, [1, 32 * Q], F32)
            eflat = self.sb(stk, [1, 32 * Q], F32)
            e4 = self.sb(stk, [128, 512], BF16)
            gt4 = self.sb(stk, [128, 512], BF16)
            ebc = self.sb(stk, [128, 256], F32)
            ytmp = self.sb(stk, [128, 256], F32)
        wxtb = [self.sb(stk, [128, 512], BF16) for _ in range(2)]
        pbc = self.sb(stk, [64, 32], F32)
        pflat = self.sb(stk, [1, 32], F32)
        hnb = self.Hn2b
        Hn = self.Hn
        rA = self.rowsA
        em, wt, dma = Ss.emit, Ss.wait, Ss.dma
        Ss.emit("vector", lambda e: e.memset(Bz[:], 0.0), inc="v")
        Ss.barrier(("scalar", "sync", "gpsimd"))
        for i in range(NCH):
            i0 = i * Q
            base_scd = Ss.now("scd")
            if not state_only:
                for g in range(8):
                    sh = (g % 2) * 64
                    for par in range(2):
                        tile = g * 2 + par
                        dma("scalar",
                            Bz[par * 64:par * 64 + 64, tile * Q:(tile + 1) * Q],
                            self.xbc[sh:sh + 64,
                                     (4 + g // 2) * T + i0:(4 + g // 2) * T + i0 + Q],
                            "scd")
            if not state_only:
                dma("scalar", lflat[0:1].rearrange("p (h q) -> p h q", h=32),
                    rA[0:32, 0 * T + i0:0 * T + i0 + Q], "scd")
                dma("scalar", eflat[0:1].rearrange("p (h q) -> p h q", h=32),
                    rA[0:32, 1 * T + i0:1 * T + i0 + Q], "scd")
            dma("scalar", pflat[0:1].rearrange("p (h q) -> p h q", h=32),
                rA[0:32, 1 * T + i0 + Q - 1:1 * T + i0 + Q], "scd")
            scd_done = Ss.now("scd")
            vrel = Ss.now("v")
            wt("tensor", "v", vrel)
            slots = []
            for t4 in range(4):
                for src_c, dstt in ((t4, Xt[t4]), (4 + t4, Bt[t4])):
                    sl = len(slots)
                    em("tensor", lambda e, sc=src_c, ii=i0, s=sl: e.transpose(
                        self.pbT[:, s * 128:s * 128 + 128],
                        self.xbc[:, sc * T + ii:sc * T + ii + Q], self.identb[:]),
                        inc=("t" if sl == 7 else None))
                    slots.append(dstt)
            wt("vector", "t", Ss.now("t"))
            for sl, dstt in enumerate(slots):
                vrel = em("vector", lambda e, d=dstt, s=sl: e.tensor_copy(
                    d[:], self.pbT[:, s * 128:s * 128 + 128]),
                    inc=("v" if sl == 7 else None))
            em("tensor", lambda e, ii=i0: e.transpose(
                self.pb[6][:, 0:32], rA[0:32, 2 * T + ii:2 * T + ii + Q],
                self.identf[0:32, 0:32]), inc="t")
            wt("vector", "t", Ss.now("t"))
            vrel = em("vector", lambda e: e.tensor_copy(wld[:, 0:32],
                                                        self.pb[6][:, 0:32]),
                      inc="v")
            wt("tensor", "v", vrel)
            em("tensor", lambda e, ii=i0: e.transpose(
                self.pb[6][:, 32:64], rA[0:32, 3 * T + ii:3 * T + ii + Q],
                self.identf[0:32, 0:32]), inc="t")
            wt("vector", "t", Ss.now("t"))
            vrel = em("vector", lambda e: e.tensor_copy(wld[:, 32:64],
                                                        self.pb[6][:, 32:64]),
                      inc="v")
            wt("tensor", "scd", scd_done)
            wt("tensor", "v", vrel)
            em("tensor", lambda e: e.matmul(
                self.pb[6][0:64, 64:96], self.ones128[0:1, 0:64],
                pflat[0:1, 0:32], start=True, stop=True), inc="t")
            wt("vector", "t", Ss.now("t"))
            vrel = em("vector", lambda e: e.tensor_copy(
                pbc[:], self.pb[6][0:64, 64:96]), inc="v")
            wt("scalar", "v", vrel)
            hnb4 = hnb[:].rearrange("p (h2 two n) -> p h2 two n", two=2, n=64)
            hn4 = Hn[:].rearrange("p (h2 two n) -> p h2 two n", two=2, n=64)
            v0_ = em("vector", lambda e: e.memset(hnb[:], 0.0), inc="v")
            wt("scalar", "v", v0_)
            a0 = em("scalar", lambda e: e.activation(
                hnb4[0:64, :, 0:1, :], hn4[:, :, 0:1, :], AF.Identity), inc="a")
            wt("gpsimd", "a", a0)
            wt("gpsimd", "v", v0_)
            gcnt = Ss.dma("gpsimd", hnb4[64:128, :, 1:2, :], hn4[:, :, 1:2, :],
                          "gpd")
            wt("tensor", "gpd", gcnt)
            wt("tensor", "a", a0)
            for grp in range(8):
                g = grp
                h0 = grp * 4
                xsl = self.xbc[(g % 2) * 64:(g % 2) * 64 + 64,
                               (g // 2) * T + i0:(g // 2) * T + i0 + Q]
                xtsl = Xt[g // 2][:, (g % 2) * 64:(g % 2) * 64 + 64]
                btsl = Bt[g // 2][:, (g % 2) * 64:(g % 2) * 64 + 64]
                if not state_only and SCAN_PART >= 1:
                    for hh in range(4):
                        em("tensor", lambda e, h=h0 + hh, cc=hh: e.matmul(
                            self.pb[0][:, cc * 128:cc * 128 + 128], self.ones128,
                            lflat[0:1, h * Q:(h + 1) * Q], start=True, stop=True),
                            inc=("t" if hh == 3 else None))
                    wt("vector", "t", Ss.now("t"))
                    for hh in range(4):
                        em("vector", lambda e, h=h0 + hh, cc=hh: e.tensor_scalar(
                            e4[:, cc * 128:cc * 128 + 128],
                            self.pb[0][:, cc * 128:cc * 128 + 128],
                            wld[:, 32 + h:33 + h], CLAMP, OP.add, OP.min),
                            inc=("v" if hh == 3 else None))
                    wt("scalar", "v", Ss.now("v"))
                    acnt = em("scalar", lambda e: e.activation(e4[:], e4[:], AF.Exp),
                              inc="a")
                    wt("vector", "a", acnt)
                    em("vector", lambda e: e.tensor_tensor(e4[:], e4[:],
                                                           self.tri4[:], OP.mult),
                       inc="v")
                if not state_only and SCAN_PART >= 2:
                    for hh in range(4):
                        h = h0 + hh
                        tile = g * 2 + (h % 2)
                        bsl = Bz[:, tile * Q:(tile + 1) * Q]
                        csl = self.xbc[:, (8 + h // 2) * T + i0:
                                       (8 + h // 2) * T + i0 + Q]
                        em("tensor", lambda e, b=bsl, cs=csl, cc=hh:
                           e.matmul(self.pb[1][:, cc * 128:cc * 128 + 128], b, cs,
                                    start=True, stop=True),
                           inc=("t" if hh == 3 else None))
                    wt("vector", "t", Ss.now("t"))
                    wt("vector", "a", Ss.now("a"))
                    if SCAN_PART >= 20:
                        vcnt = em("vector", lambda e: e.tensor_tensor(
                            gt4[:], self.pb[1][:, 0:512], e4[:], OP.mult),
                            inc="v")
                        wt("tensor", "v", vcnt)
                if not state_only and SCAN_PART >= 3:
                    for hh in range(4):
                        h = h0 + hh
                        cb = (h % 2)
                        slot = (hh // 2) * 128
                        em("tensor", lambda e, xt=xtsl, cc=hh, sl=slot, base=cb * 64:
                           e.matmul(self.pb[2][base:base + 64, sl:sl + 128], xt,
                                    gt4[:, cc * 128:cc * 128 + 128],
                                    start=True, stop=False))
                        em("tensor", lambda e, h_=h, sl=slot, base=cb * 64,
                           xs=self.xbc[:, (g // 2) * T + i0:(g // 2) * T + i0 + Q]:
                           e.matmul(self.pb[2][base:base + 64, sl:sl + 128],
                                    self.ddiag[:, h_ * 64:h_ * 64 + 64],
                                    xs, start=False, stop=True))
                        csl = self.xbc[:, (8 + h // 2) * T + i0:
                                       (8 + h // 2) * T + i0 + Q]
                        em("tensor", lambda e, h_=h, sl=slot, base=cb * 64, cs=csl:
                           e.matmul(self.pb[3][base:base + 64, sl:sl + 128],
                                    hnb[:, h_ * 64:h_ * 64 + 64], cs,
                                    start=True, stop=True))
                        em("tensor", lambda e, h_=h, sl=slot, top=(hh % 2 == 0):
                           e.matmul(self.pb[0][:, sl:sl + 128],
                                    self.top64 if top else self.bot64,
                                    eflat[0:1, h_ * Q:(h_ + 1) * Q],
                                    start=top, stop=not top),
                           inc=("t" if hh == 3 else None))
                    tnow = Ss.now("t")
                if not state_only and SCAN_PART >= 4:
                    tnow = Ss.now("t")
                    wt("vector", "t", tnow)
                    em("vector", lambda e: e.tensor_copy(ebc[:], self.pb[0][:, 0:256]))
                    em("vector", lambda e: e.tensor_tensor(
                        ytmp[:], self.pb[3][:, 0:256], ebc[:], OP.mult))
                    for sl2 in range(2):
                        ch = 2 * g + sl2
                        vcnt = em("vector", lambda e, c=ch, s=sl2, ii=i0:
                                  e.tensor_tensor(
                                      self.ya[:, c * T + ii:c * T + ii + Q],
                                      ytmp[:, s * 128:s * 128 + 128],
                                      self.pb[2][:, s * 128:s * 128 + 128],
                                      OP.add),
                            inc="v")
                    wt("tensor", "v", vcnt)
            wxfree = [None, None]
            hnfree = [None, None]
            for r in range(4):
                if wxfree[r % 2] is not None:
                    wt("scalar", "t", wxfree[r % 2])
                atok = None
                for hh in range(8):
                    h = 8 * r + hh
                    g = h // 4
                    xts = Xt[g // 2][:, (g % 2) * 64:(g % 2) * 64 + 64]
                    atok = em("scalar", lambda e, h_=h, xt=xts, cc=hh,
                              b=wxtb[r % 2]: e.activation(
                                  b[:, cc * 64:cc * 64 + 64], xt, AF.Identity,
                                  scale=wld[:, h_:h_ + 1]), inc="a")
                wt("tensor", "a", atok)
                if hnfree[r % 2] is not None:
                    wt("tensor", "v", hnfree[r % 2])
                pbR = self.pb[4 + r % 2]
                for hh in range(8):
                    h = 8 * r + hh
                    g = h // 4
                    bts = Bt[g // 2][:, (g % 2) * 64:(g % 2) * 64 + 64]
                    em("tensor", lambda e, bt=bts, cc=hh, b=wxtb[r % 2],
                       p=pbR: e.matmul(
                           p[0:64, cc * 64:cc * 64 + 64], bt,
                           b[:, cc * 64:cc * 64 + 64],
                           start=True, stop=True),
                       inc=("t" if hh == 7 else None))
                wxfree[r % 2] = Ss.now("t")
                wt("vector", "t", wxfree[r % 2])
                vcnt = None
                for hh in range(8):
                    h = 8 * r + hh
                    vcnt = em("vector", lambda e, h_=h, cc=hh, p=pbR:
                              e.scalar_tensor_tensor(
                                  Hn[:, h_ * 64:h_ * 64 + 64],
                                  Hn[:, h_ * 64:h_ * 64 + 64],
                                  pbc[:, h_:h_ + 1],
                                  p[0:64, cc * 64:cc * 64 + 64],
                                  OP.mult, OP.add),
                              inc=("v" if hh == 7 else None))
                hnfree[r % 2] = vcnt
            Ss.barrier(("tensor", "vector", "scalar"))

    def rms_stats_res(self, arena, stride, nch, rout, stk):
        Ss = self.S
        sq = self.sb(stk, [128, T], F32)
        scr = self.sb(stk, [1, T], F32)
        epsc = self.sb(stk, [1, 1], F32)
        Ss.emit("vector", lambda e: e.memset(epsc[:], EPS), inc="v")
        tprev = 0
        for c in range(nch):
            if c >= 1:
                Ss.wait("scalar", "t", tprev)
            acnt = Ss.emit("scalar", lambda e, cc=c: e.activation(
                sq[:], arena[:, cc * stride:cc * stride + T], AF.Square), inc="a")
            Ss.wait("tensor", "a", acnt)
            for si, (s0, sz) in enumerate(NTS):
                Ss.emit("tensor", lambda e, p=self.pb[5 + si], s0_=s0, sz_=sz,
                        st=(c == 0), sp=(c == nch - 1):
                        e.matmul(p[0:1, 0:sz_], self.onescol[:, 0:1],
                                 sq[:, s0_:s0_ + sz_], start=st, stop=sp),
                        inc="t")
            tprev = Ss.now("t")
        Ss.wait("vector", "t", tprev)
        for si, (s0, sz) in enumerate(NTS):
            Ss.emit("vector", lambda e, p=self.pb[5 + si], s0_=s0, sz_=sz:
                    e.tensor_copy(scr[0:1, s0_:s0_ + sz_], p[0:1, 0:sz_]),
                    inc="v")
        em, wt = Ss.emit, Ss.wait
        v = em("vector", lambda e: e.tensor_single_scalar(
            scr[0:1, 0:T], scr[0:1, 0:T], 1.0 / 2048.0, OP.mult), inc="v")
        wt("scalar", "v", v)
        a = em("scalar", lambda e: e.activation(scr[0:1, 0:T], scr[0:1, 0:T],
                                                AF.Sqrt, bias=epsc[0:1, 0:1]),
               inc="a")
        wt("vector", "a", a)
        v = em("vector", lambda e: e.reciprocal(rout, scr[0:1, 0:T]), inc="v")
        return ("v", v)

    def ln_stats_res(self, arena, stride, mout, rout, stk):
        Ss = self.S
        sq = self.sb(stk, [128, T], F32)
        scr = self.sb(stk, [1, 2 * T], F32)
        epsc = self.sb(stk, [1, 1], F32)
        Ss.emit("vector", lambda e: e.memset(epsc[:], EPS), inc="v")
        ps_s = [self.pb[2], self.pb[3]]
        ps_q = [self.pb[5], self.pb[6]]
        tprev = 0
        for c in range(16):
            if c >= 1:
                Ss.wait("scalar", "t", tprev)
            acnt = Ss.emit("scalar", lambda e, cc=c: e.activation(
                sq[:], arena[:, cc * stride:cc * stride + T], AF.Square), inc="a")
            Ss.wait("tensor", "a", acnt)
            for si, (s0, sz) in enumerate(NTS):
                for psm, gs in ((ps_s[si], 0), (ps_q[si], 1)):
                    Ss.emit("tensor", lambda e, p=psm, g=gs, cc=c, s0_=s0,
                            sz_=sz, st=(c == 0), sp=(c == 15):
                            e.matmul(p[0:1, 0:sz_], self.onescol[:, 0:1],
                                     (arena[:, cc * stride + s0_:
                                            cc * stride + s0_ + sz_]
                                      if g == 0 else sq[:, s0_:s0_ + sz_]),
                                     start=st, stop=sp),
                            inc="t")
            tprev = Ss.now("t")
        Ss.wait("vector", "t", tprev)
        for si, (s0, sz) in enumerate(NTS):
            Ss.emit("vector", lambda e, p=ps_s[si], s0_=s0, sz_=sz:
                    e.tensor_copy(scr[0:1, s0_:s0_ + sz_], p[0:1, 0:sz_]))
            Ss.emit("vector", lambda e, p=ps_q[si], s0_=s0, sz_=sz:
                    e.tensor_copy(scr[0:1, T + s0_:T + s0_ + sz_],
                                  p[0:1, 0:sz_]), inc="v")
        s1 = scr[0:1, 0:T]
        s2 = scr[0:1, T:2 * T]
        em, wt = Ss.emit, Ss.wait
        inv_d = 1.0 / 2048.0
        em("vector", lambda e: e.tensor_single_scalar(mout, s1, inv_d, OP.mult))
        em("vector", lambda e: e.tensor_single_scalar(s2, s2, inv_d, OP.mult))
        em("vector", lambda e: e.tensor_tensor(s1, mout, mout, OP.mult))
        v = em("vector", lambda e: e.tensor_tensor(s2, s2, s1, OP.subtract),
               inc="v")
        wt("scalar", "v", v)
        a = em("scalar", lambda e: e.activation(s1, s2, AF.Sqrt,
                                                bias=epsc[0:1, 0:1]), inc="a")
        wt("vector", "a", a)
        v = em("vector", lambda e: e.reciprocal(rout, s1), inc="v")
        return ("v", v)

    def p3(self, stk):
        Ss = self.S
        for c in range(16):
            Ss.emit("vector", lambda e, cc=c: e.tensor_tensor(
                self.ya[:, cc * T:(cc + 1) * T], self.ya[:, cc * T:(cc + 1) * T],
                self.siluz[:, cc * T:(cc + 1) * T], OP.mult),
                inc=("v" if c == 15 else None))
        Ss.barrier()
        rho3 = self.sb(stk, [1, T], F32)
        with ExitStack() as sub:
            self.rms_stats_res(self.ya, T, 16, rho3[0:1, 0:T], sub)
        Ss.barrier()
        rbc3 = self.sb(stk, [128, T], F32)
        self.bcast_row(rho3[0:1, 0:T], rbc3, self.pb[0], T)
        Ss.barrier()
        tmpf = self.sb(stk, [128, T], F32)
        aprev = 0
        for c in range(16):
            if c >= 1:
                Ss.wait("vector", "a", aprev)
            v = Ss.emit("vector", lambda e, cc=c: e.tensor_tensor(
                tmpf[:], self.ya[:, cc * T:(cc + 1) * T], rbc3[:], OP.mult),
                inc="v")
            Ss.wait("scalar", "v", v)
            aprev = Ss.emit("scalar", lambda e, cc=c: e.activation(
                self.ya[:, cc * T:(cc + 1) * T], tmpf[:], AF.Identity,
                scale=self.ssmw[:, cc:cc + 1]), inc="a")
        Ss.barrier()
        mbc = self.sb(stk, [128, T], F32)
        rbc = self.sb(stk, [128, T], F32)
        self.bcast_row(self.stat_main[0:1, 3:3 + T], mbc, self.pb[0], T)
        self.bcast_row(self.stat_main[0:1, TN + 3:TN + 3 + T], rbc, self.pb[0], T)
        Ss.barrier()
        wbufs = [self.sb(stk, [128, 2048], BF16) for _ in range(2)]
        xb = [self.sb(stk, [128, T], F32) for _ in range(2)]
        ut = self.sb(stk, [128, 512], F32)
        xsrc = self.hbm["xT"]
        st2x = Stream2(self, "sync", "p3x")
        st2x.load(0, xb[0][:], xsrc[0:128, 3:TN])
        st2x.load(1, xb[1][:], xsrc[128:256, 3:TN])

        def rhs_fn(k, n0, nsz):
            return self.ya[:, k * T + n0:k * T + n0 + nsz]

        def evac(m, ni, n0, nsz, ps, tdone):
            st2x.wait("vector", m % 2)
            Ss.wait("vector", "t", tdone)
            Ss.emit("vector", lambda e, b=xb[m % 2], nn0=n0, nns=nsz:
                    e.tensor_tensor(ut[:, 0:nns], b[:, nn0:nn0 + nns],
                                    mbc[:, nn0:nn0 + nns], OP.subtract))
            Ss.emit("vector", lambda e, nn0=n0, nns=nsz:
                    e.tensor_tensor(ut[:, 0:nns], ut[:, 0:nns],
                                    rbc[:, nn0:nn0 + nns], OP.mult))
            cnt = Ss.emit("vector", lambda e, p=ps, mm=m, nn0=n0, nns=nsz:
                          e.tensor_tensor(
                              self.r2a[:, mm * T + nn0:mm * T + nn0 + nns],
                              ut[:, 0:nns], p[:, 0:nns], OP.add), inc="v")
            if ni == 1 and m + 2 < 16:
                st2x.load(m % 2, xb[m % 2][:],
                          xsrc[(m + 2) * 128:(m + 3) * 128, 3:TN],
                          guard=("v", cnt))
            return ("v", cnt)

        self.stream_mm("woutT", list(range(16)), rhs_fn, evac,
                       (self.pb[0], self.pb[1]), wbufs, 16)
        Ss.barrier()

    def p4(self, stk):
        Ss = self.S
        with ExitStack() as sub:
            rho4 = self.sb(sub, [1, T], F32)
            with ExitStack() as sub2:
                self.rms_stats_res(self.r2a, T, 16, rho4[0:1, 0:T], sub2)
            Ss.barrier()
            rbc4 = self.sb(sub, [128, T], F32)
            self.bcast_row(rho4[0:1, 0:T], rbc4, self.pb[0], T)
            Ss.barrier()
            tmpf = self.sb(sub, [128, T], F32)
            aprev = 0
            for c in range(16):
                if c >= 1:
                    Ss.wait("vector", "a", aprev)
                v = Ss.emit("vector", lambda e, cc=c: e.tensor_tensor(
                    tmpf[:], self.r2a[:, cc * T:(cc + 1) * T], rbc4[:], OP.mult),
                    inc="v")
                Ss.wait("scalar", "v", v)
                aprev = Ss.emit("scalar", lambda e, cc=c: e.activation(
                    self.rms2[:, cc * T:(cc + 1) * T], tmpf[:], AF.Identity,
                    scale=self.wpost[:, cc:cc + 1]), inc="a")
            Ss.barrier()
            xb = [self.sb(sub, [128, T], F32) for _ in range(2)]
            xsrc = self.hbm["xT"]
            st2x = Stream2(self, "sync", "p4x")
            st2x.load(0, xb[0][:], xsrc[0:128, 3:TN])
            st2x.load(1, xb[1][:], xsrc[128:256, 3:TN])
            for c in range(16):
                st2x.wait("vector", c % 2)
                v = Ss.emit("vector", lambda e, cc=c, b=xb[c % 2]:
                            e.tensor_tensor(
                                self.r2a[:, cc * T:(cc + 1) * T],
                                self.r2a[:, cc * T:(cc + 1) * T],
                                b[:], OP.add), inc="v")
                if c + 2 < 16:
                    st2x.load(c % 2, xb[c % 2][:],
                              xsrc[(c + 2) * 128:(c + 3) * 128, 3:TN],
                              guard=("v", v))
            Ss.barrier()
        g8a = self.sb(stk, [128, 16 * T], BF16)
        g8b = self.sb(stk, [128, 16 * T], BF16)

        def g8c(c):
            return (g8a if c < 16 else g8b, (c % 16) * T)
        wg = [self.sb(stk, [128, 2048], BF16) for _ in range(2)]
        wu = [self.sb(stk, [128, 2048], BF16) for _ in range(2)]
        wd = [self.sb(stk, [128, 4096], BF16) for _ in range(2)]
        stile = self.sb(stk, [128, 512], BF16)
        for half in range(2):
            self._swiglu_gu(half, g8c, wg, wu, stk, stile)
            Ss.barrier()
            self._down_half(half, g8c, wd, "wdT", None)
            Ss.barrier()

    def _swiglu_gu(self, half, g8, wg, wu, stk, stile):
        Ss = self.S
        gap = self.hbm["wgT"]
        uap = self.hbm["wuT"]
        m_list = list(range(half * 32, (half + 1) * 32))
        st2 = Stream2(self, "gpsimd", f"wg{self.uid()}_")
        buf_use = [0, 0]

        def load(mi):
            g = ("t", buf_use[mi % 2]) if buf_use[mi % 2] else None
            st2.load(mi % 2, wg[mi % 2][:], gap[m_list[mi]], guard=g)
            st2.load(mi % 2, wu[mi % 2][:], uap[m_list[mi]])
        load(0)
        if len(m_list) > 1:
            load(1)
        ps_rel = None
        for mi, m in enumerate(m_list):
            st2.wait("tensor", mi % 2)
            for ni, (n0, nsz) in enumerate(NTS):
                if ps_rel is not None:
                    Ss.wait("tensor", *ps_rel)
                for pj, wb in ((self.pb[0], wg[mi % 2]), (self.pb[1], wu[mi % 2])):
                    for j in range(16):
                        Ss.emit("tensor", lambda e, p=pj, w=wb, jj=j, nn0=n0,
                                nns=nsz, st=(j == 0), sp=(j == 15):
                                e.matmul(p[:, 0:nns], w[:, jj * 128:(jj + 1) * 128],
                                         self.rms2[:, jj * T + nn0:jj * T + nn0 + nns],
                                         start=st, stop=sp),
                                inc=("t" if j == 15 else None))
                tdone = Ss.now("t")
                buf_use[mi % 2] = tdone
                Ss.wait("scalar", "t", tdone)
                a = Ss.emit("scalar", lambda e, nns=nsz: e.activation(
                    stile[:, 0:nns], self.pb[0][:, 0:nns], AF.Silu), inc="a")
                Ss.wait("vector", "a", a)
                gt_, go_ = g8(m - half * 32)
                cnt = Ss.emit("vector", lambda e, gg=gt_, oo=go_, nn0=n0, nns=nsz:
                              e.tensor_tensor(
                                  gg[:, oo + nn0:oo + nn0 + nns],
                                  stile[:, 0:nns], self.pb[1][:, 0:nns], OP.mult),
                              inc="v")
                ps_rel = ("v", cnt)
            if mi + 2 < len(m_list):
                load(mi + 2)

    def _down_half(self, half, g8, wd, wname, scale_bc):
        Ss = self.S
        wap = self.hbm[wname]
        st2 = Stream2(self, "gpsimd", f"wd{self.uid()}_")
        buf_use = [0, 0]

        def load2(mi):
            g = ("t", buf_use[mi % 2]) if buf_use[mi % 2] else None
            st2.load(mi % 2, wd[mi % 2][:],
                     wap[mi, :, half * 4096:(half + 1) * 4096], guard=g)
        load2(0)
        load2(1)
        ps_rel = [None, None]
        it = 0
        for m in range(16):
            st2.wait("tensor", m % 2)
            for ni, (n0, nsz) in enumerate(NTS):
                pi = it % 2
                ps = self.pb[pi]
                if ps_rel[pi] is not None:
                    Ss.wait("tensor", *ps_rel[pi])
                for j in range(32):
                    gt_, go_ = g8(j)
                    Ss.emit("tensor", lambda e, p=ps, w=wd[m % 2], jj=j, nn0=n0,
                            nns=nsz, gg=gt_, oo=go_, st=(j == 0), sp=(j == 31):
                            e.matmul(p[:, 0:nns], w[:, jj * 128:(jj + 1) * 128],
                                     gg[:, oo + nn0:oo + nn0 + nns],
                                     start=st, stop=sp),
                            inc=("t" if j == 31 else None))
                tdone = Ss.now("t")
                buf_use[m % 2] = tdone
                Ss.wait("vector", "t", tdone)
                cnt = Ss.emit("vector", lambda e, p=ps, mm=m, nn0=n0, nns=nsz:
                              e.tensor_tensor(
                                  self.r2a[:, mm * T + nn0:mm * T + nn0 + nns],
                                  self.r2a[:, mm * T + nn0:mm * T + nn0 + nns],
                                  p[:, 0:nns], OP.add), inc="v")
                ps_rel[pi] = ("v", cnt)
                it += 1
            if m + 2 < 16:
                load2(m + 2)

    def ladder(self, src_psum_or_sbuf, scbc, n0, nsz, dst_ap, v0, vt, ct,
               from_psum=True):
        Ss = self.S
        em, wt = Ss.emit, Ss.wait
        em("vector", lambda e: e.tensor_tensor(
            v0[:, 0:nsz], src_psum_or_sbuf, scbc, OP.mult))
        em("vector", lambda e: e.tensor_copy(vt[:, 0:nsz], v0[:, 0:nsz]))
        for j, k in enumerate((128, 64, 32, 16, 8, 4, 2, 1)):
            em("vector", lambda e, jj=j, kk=k: e.tensor_scalar(
                ct[:, 0:nsz], vt[:, 0:nsz], self.ladthr[:, jj:jj + 1],
                -float(kk), OP.is_ge, OP.mult))
            em("vector", lambda e: e.tensor_tensor(
                vt[:, 0:nsz], vt[:, 0:nsz], ct[:, 0:nsz], OP.add))
        cnt = em("vector", lambda e: e.scalar_tensor_tensor(
            dst_ap, v0[:, 0:nsz], 128.0, vt[:, 0:nsz], OP.subtract, OP.subtract),
            inc="v")
        return ("v", cnt)

    def ladder2_pair(self, srcs, sccols, dsts, bufs, ptoks):
        Ss = self.S
        em, wt = Ss.emit, Ss.wait
        N = len(srcs)
        cur = [None] * N
        first = [None] * N
        for i in range(N):
            wt("vector", *ptoks[i])
            first[i] = em("vector", lambda e, ii=i: e.tensor_tensor(
                bufs[ii][0][:], srcs[ii], sccols[ii], OP.mult), inc="v")
            cur[i] = ("v", first[i])
        prev_a = [None] * N
        for j, k in enumerate(LAD_KS):
            if j in LAD_ACT:
                atok = [None] * N
                for i in range(N):
                    v0, wtb, ct = bufs[i]
                    s = v0 if j == 0 else wtb
                    wt("scalar", *cur[i])
                    atok[i] = em("scalar", lambda e, ss=s, cc=ct, jj=j:
                                 e.activation(cc[:], ss[:], AF.Sign,
                                              bias=self.ladthr[:, 16 + jj:17 + jj]),
                                 inc="a")
                    prev_a[i] = atok[i]
                for i in range(N):
                    v0, wtb, ct = bufs[i]
                    s = v0 if j == 0 else wtb
                    wt("vector", "a", atok[i])
                    if cur[i][0] == "g":
                        wt("vector", *cur[i])
                    cur[i] = ("v", em(
                        "vector", lambda e, ss=s, ww=wtb, cc=ct, kk=k:
                        e.scalar_tensor_tensor(ww[:], cc[:], -0.5 * kk,
                                               ss[:], OP.mult, OP.add),
                        inc="v"))
            else:
                vtok = [None] * N
                for i in range(N):
                    v0, wtb, ct = bufs[i]
                    s = v0 if j == 0 else wtb
                    if cur[i][0] == "g":
                        wt("vector", *cur[i])
                    vtok[i] = em("vector", lambda e, ss=s, cc=ct, jj=j, kk=k:
                                 e.tensor_scalar(cc[:], ss[:],
                                                 self.ladthr[:, 8 + jj:9 + jj],
                                                 -float(kk), OP.is_ge, OP.mult),
                                 inc="v")
                for i in range(N):
                    v0, wtb, ct = bufs[i]
                    s = v0 if j == 0 else wtb
                    wt("gpsimd", "v", vtok[i])
                    if prev_a[i] is not None:
                        wt("gpsimd", "a", prev_a[i])
                    cur[i] = ("g", em(
                        "gpsimd", lambda e, ss=s, ww=wtb, cc=ct:
                        e.tensor_tensor(ww[:], ss[:], cc[:], OP.add),
                        inc="g"))
        fin = None
        for i in range(N):
            v0, wtb, ct = bufs[i]
            if cur[i][0] == "g":
                wt("vector", *cur[i])
            fin = em("vector", lambda e, vv=v0, ww=wtb, dd=dsts[i]:
                     e.scalar_tensor_tensor(dd, vv[:], LAD_FC, ww[:],
                                            OP.subtract, OP.subtract), inc="v")
        return first, fin

    def token_max_row(self, runmx, runmn, mrow, scrow2, stk):
        Ss = self.S
        mcol = self.sb(stk, [128, 16], F32)
        m8 = self.sb(stk, [16, 128], F32)
        vrel = Ss.now("v")
        for tb in range(8):
            for pi, (src_t, red) in enumerate(((runmx, 0), (runmn, 1))):
                Ss.wait("tensor", "v", vrel)
                Ss.emit("tensor", lambda e, b=tb, s=src_t: e.transpose(
                    self.pb[0][:, 0:128], s[:, b * 128:(b + 1) * 128],
                    self.identf[:]), inc="t")
                Ss.wait("vector", "t", Ss.now("t"))
                if red == 0:
                    vrel = Ss.emit("vector", lambda e, b=tb: e.reduce_max(
                        mcol[:, b:b + 1], self.pb[0][:, 0:128],
                        axis=mybir.AxisListType.X), inc="v")
                else:
                    vrel = Ss.emit("vector", lambda e, b=tb: e.tensor_reduce(
                        mcol[:, 8 + b:9 + b], self.pb[0][:, 0:128],
                        mybir.AxisListType.X, OP.min), inc="v")
        Ss.wait("tensor", "v", vrel)
        Ss.emit("tensor", lambda e: e.transpose(
            self.pb[0][0:16, 0:128], mcol[:, 0:16], self.identf[:]), inc="t")
        Ss.wait("vector", "t", Ss.now("t"))
        v = Ss.emit("vector", lambda e: e.tensor_copy(m8[:],
                                                      self.pb[0][0:16, 0:128]),
                    inc="v")
        Ss.wait("scalar", "v", v)
        Ss.dma("scalar", mrow[0:1].rearrange("p (tb q) -> p tb q", tb=8),
               m8[0:8, :], "scd")
        Ss.dma("scalar", scrow2[0:1].rearrange("p (tb q) -> p tb q", tb=8),
               m8[8:16, :], "scd")
        Ss.wait("vector", "scd", Ss.now("scd"))
        Ss.emit("vector", lambda e: e.tensor_single_scalar(
            scrow2, scrow2, -1.0, OP.mult))
        v = Ss.emit("vector", lambda e: e.tensor_tensor(
            mrow[0:1, 0:T], mrow[0:1, 0:T], scrow2, OP.max), inc="v")

    def p5(self, stk, sC, sXQ):
        Ss = self.S
        sLN2 = ExitStack()
        ln2 = self.sb(sLN2, [128, 16 * T], BF16, "ln2")
        stat = self.sb(stk, [1, 2 * T], F32)
        with ExitStack() as l:
            self.ln_stats_res(self.r2a, T, stat[0:1, 0:T], stat[0:1, T:2 * T], l)
        Ss.barrier()
        mbc = self.sb(stk, [128, T], F32)
        rbc = self.sb(stk, [128, T], F32)
        self.bcast_row(stat[0:1, 0:T], mbc, self.pb[0], T)
        self.bcast_row(stat[0:1, T:2 * T], rbc, self.pb[0], T)
        Ss.barrier()
        tmpf = self.sb(stk, [128, T], F32)
        for c in range(16):
            Ss.emit("vector", lambda e, cc=c: e.tensor_tensor(
                tmpf[:], self.r2a[:, cc * T:(cc + 1) * T], mbc[:], OP.subtract))
            v = Ss.emit("vector", lambda e, cc=c: e.tensor_tensor(
                ln2[:, cc * T:(cc + 1) * T], tmpf[:], rbc[:], OP.mult), inc="v")
            Ss.wait("sync", "v", v)
            Ss.dma("sync", self.hbm["sch2"][c * 128:(c + 1) * 128, :],
                   self.r2a[:, c * T:(c + 1) * T], "st")
        Ss.barrier()
        sC.close()
        Y1 = self.sb(stk, [128, 16 * T], F32, "Y1")
        self.xq = self.sb(sXQ, [128, 16 * T], BF16, "xq", hi=True)
        xq = self.xq
        H16 = _hadamard(16)
        runmx = self.sb(stk, [128, T], F32)
        runmn = self.sb(stk, [128, T], F32)
        Ss.emit("vector", lambda e: e.memset(runmx[:], 0.0))
        Ss.emit("vector", lambda e: e.memset(runmn[:], 0.0), inc="v")
        Ss.barrier(("tensor", "vector"))
        ps_rel = [None, None]
        it = 0
        for ip in range(16):
            order = sorted(range(16), key=lambda k: -H16[ip, k])
            for ni, (n0, nsz) in enumerate(NTS):
                pi = it % 2
                ps = self.pb[pi]
                if ps_rel[pi] is not None:
                    Ss.wait("tensor", *ps_rel[pi])
                for j, k in enumerate(order):
                    plus = H16[ip, k] > 0
                    Ss.emit("tensor", lambda e, p=ps, kk=k, pl=plus, nn0=n0,
                            nns=nsz, st=(j == 0), sp=(j == 15):
                            e.matmul(p[:, 0:nns],
                                     self.h128pm[:, 0:128] if pl
                                     else self.h128pm[:, 128:256],
                                     ln2[:, kk * T + nn0:kk * T + nn0 + nns],
                                     start=st, stop=sp),
                            inc=("t" if j == 15 else None))
                tdone = Ss.now("t")
                Ss.wait("vector", "t", tdone)
                Ss.emit("vector", lambda e, p=ps, nn0=n0, nns=nsz:
                        e.tensor_tensor(runmx[:, nn0:nn0 + nns],
                                        runmx[:, nn0:nn0 + nns],
                                        p[:, 0:nns], OP.max))
                Ss.emit("vector", lambda e, p=ps, nn0=n0, nns=nsz:
                        e.tensor_tensor(runmn[:, nn0:nn0 + nns],
                                        runmn[:, nn0:nn0 + nns],
                                        p[:, 0:nns], OP.min))
                cnt = Ss.emit("vector", lambda e, p=ps, mm=ip, nn0=n0, nns=nsz:
                              e.tensor_copy(Y1[:, mm * T + nn0:mm * T + nn0 + nns],
                                            p[:, 0:nns]), inc="v")
                ps_rel[pi] = ("v", cnt)
                it += 1
        Ss.barrier()
        sLN2.close()
        self.sSCROW = ExitStack()
        self.scrow = self.sb(self.sSCROW, [1, 2 * T], F32, hi=True)
        mrow = self.sb(stk, [1, 3 * T], F32)
        self.token_max_row(runmx, runmn, mrow[0:1, 0:T],
                           mrow[0:1, 2 * T:3 * T], stk)
        Ss.barrier(("vector",))
        rn = float(np.float32(np.sqrt(2048.0)))
        deq1 = self.scrow[0:1, 0:T]
        sc1 = mrow[0:1, T:2 * T]
        tmpr = mrow[0:1, 2 * T:3 * T]
        em = Ss.emit
        em("vector", lambda e: e.tensor_scalar(deq1, mrow[0:1, 0:T],
                                               self.c_invrn1, EPS,
                                               OP.mult, OP.max))
        em("vector", lambda e: e.tensor_single_scalar(tmpr, deq1, rn, OP.mult))
        em("vector", lambda e: e.reciprocal(sc1, tmpr))
        em("vector", lambda e: e.tensor_single_scalar(sc1, sc1, 127.0, OP.mult))
        v = em("vector", lambda e: e.tensor_single_scalar(deq1, deq1, 1.0 / 127.0,
                                                          OP.mult), inc="v")
        Ss.barrier()
        scbc = self.sb(stk, [128, T], F32)
        self.bcast_row(sc1, scbc, self.pb[0], T)
        Ss.barrier()
        lbufs = [tuple(self.sb(stk, [128, 512], F32) for _ in range(3))
                 for _ in range(4)]
        for c in range(0, 16, 2):
            tiles = [(c, 0), (c, 1), (c + 1, 0), (c + 1, 1)]
            self.ladder2_pair(
                [Y1[:, cc * T + h * 512:cc * T + (h + 1) * 512]
                 for cc, h in tiles],
                [scbc[:, h * 512:(h + 1) * 512] for cc, h in tiles],
                [xq[:, cc * T + h * 512:cc * T + (h + 1) * 512]
                 for cc, h in tiles],
                lbufs, [("v", 0)] * 4)
        Ss.barrier()

    def p0(self, stk):
        Ss = self.S
        col = self.sb(stk, [128, 200], F32)
        self.wsc = self.sb(self.stk, [1, 8], F32, hi=True)
        self.abc3 = self.sb(self.stk, [128, 6], F32, hi=True)
        wt_ = [self.sb(stk, [128, 2048], F32) for _ in range(2)]
        dummy = self.sb(stk, [128, 2048], F32)
        srcs = []
        for m in range(64):
            srcs.append(self.hbm["fgT"][m])
        for m in range(64):
            srcs.append(self.hbm["fuT"][m])
        for m in range(16):
            for q4 in range(4):
                srcs.append(self.hbm["fdT"][m, :, q4 * 2048:(q4 + 1) * 2048])
        st2 = Stream2(self, "sync", "p0x")
        st2.load(0, wt_[0][:], srcs[0])
        st2.load(1, wt_[1][:], srcs[1])
        vprev, aprev = 0, 0
        for j, src in enumerate(srcs):
            if j % 2 == 0:
                st2.wait("scalar", 0)
                aprev = Ss.emit("scalar", lambda e, b=wt_[j % 2], jj=j:
                                e.activation(dummy[:], b[:], AF.Abs,
                                             accum_out=col[:, jj:jj + 1]), inc="a")
            else:
                st2.wait("vector", 1)
                vprev = Ss.emit("vector", lambda e, b=wt_[j % 2], jj=j:
                                e.reduce_sum(col[:, jj:jj + 1], b[:],
                                             axis=mybir.AxisListType.X,
                                             apply_absolute_value=True), inc="v")
            if j + 2 < len(srcs):
                st2.load(j % 2, wt_[j % 2][:], srcs[j + 2],
                         guard=(("a", aprev) if j % 2 == 0 else ("v", vprev)))
        Ss.barrier()
        tot = self.sb(stk, [128, 3], F32)
        for wi, (c0, cn) in enumerate(((0, 64), (64, 64), (128, 64))):
            Ss.emit("vector", lambda e, w=wi, a=c0, b=cn: e.reduce_sum(
                tot[:, w:w + 1], col[:, a:a + b], axis=mybir.AxisListType.X),
                inc="v")
        Ss.barrier(("tensor",))
        for wi in range(3):
            Ss.emit("tensor", lambda e, w=wi: e.matmul(
                self.pb[2][0:1, w:w + 1], self.onescol[:, 0:1], tot[:, w:w + 1],
                start=True, stop=True), inc="t")
        Ss.barrier(("vector",))
        v = Ss.emit("vector", lambda e: e.tensor_copy(self.wsc[0:1, 0:3],
                                                      self.pb[2][0:1, 0:3]),
                    inc="v")
        Ss.emit("vector", lambda e: e.tensor_scalar(
            self.wsc[0:1, 0:3], self.wsc[0:1, 0:3], self.c_nel, EPS,
            OP.mult, OP.max))
        v = Ss.emit("vector", lambda e: e.tensor_single_scalar(
            self.wsc[0:1, 3:6], self.wsc[0:1, 0:3], 0.5, OP.mult), inc="v")
        Ss.barrier(("tensor",))
        Ss.emit("tensor", lambda e: e.matmul(
            self.pb[2][:, 0:3], self.ones128, self.wsc[0:1, 3:6],
            start=True, stop=True), inc="t")
        Ss.barrier(("vector",))
        Ss.emit("vector", lambda e: e.tensor_copy(self.abc3[:, 0:3],
                                                  self.pb[2][:, 0:3]), inc="v")
        Ss.emit("vector", lambda e: e.tensor_single_scalar(
            self.abc3[:, 3:6], self.abc3[:, 0:3], -1.0, OP.mult), inc="v")
        Ss.barrier()

    def p6(self, stk, sF, sXQ):
        Ss = self.S
        em, wt, dma = Ss.emit, Ss.wait, Ss.dma
        xq = self.xq
        stk_gu = ExitStack()
        wtg = [self.sb(stk_gu, [128, 1024], F32) for _ in range(2)]
        wtu = [self.sb(stk_gu, [128, 1024], F32) for _ in range(2)]
        tqg = [self.sb(stk_gu, [128, 2048], BF16) for _ in range(2)]
        tqu = [self.sb(stk_gu, [128, 2048], BF16) for _ in range(2)]
        sgn = self.sb(stk_gu, [128, 1024], BF16)
        stile = self.sb(stk_gu, [128, 512], BF16)
        utmp = self.sb(stk_gu, [128, 512], F32)
        stg = [self.sb(stk_gu, [128, 1024], BF16) for _ in range(2)]
        gsbc = self.sb(stk_gu, [128, T], BF16)
        usbc = self.sb(stk_gu, [128, T], BF16)
        gur = self.sb(stk_gu, [1, 2 * T], F32)
        deq1 = self.scrow[0:1, 0:T]
        em("vector", lambda e: e.tensor_scalar(gur[0:1, 0:T], deq1,
                                               self.wsc[0:1, 0:1], None, OP.mult))
        v = em("vector", lambda e: e.tensor_scalar(gur[0:1, T:2 * T], deq1,
                                                   self.wsc[0:1, 1:2], None,
                                                   OP.mult), inc="v")
        Ss.barrier(("tensor", "vector"))
        self.bcast_row(gur[0:1, 0:T], gsbc, self.pb[2], T)
        self.bcast_row(gur[0:1, T:2 * T], usbc, self.pb[2], T)
        Ss.barrier()
        gap, uap = self.hbm["fgT"], self.hbm["fuT"]
        scA = self.hbm["scA"]
        stwg = Stream2(self, "sync", "wg")
        stwu = Stream2(self, "gpsimd", "wu")
        spl = Stream2(self, "scalar", "spl")
        for s2 in range(2):
            stwg.load(s2, wtg[s2][:], gap[0, :, s2 * 1024:(s2 + 1) * 1024])
            stwu.load(s2, wtu[s2][:], uap[0, :, s2 * 1024:(s2 + 1) * 1024])
        nld = [2, 2]
        qtok = [[None, None], [None, None]]
        mmtok = [[None, None], [None, None]]
        evg = [None, None]
        evu = [None, None]
        evh = [None, None]
        stg_free = [None, None]

        def quant_tile(m):
            for wi, (wtb, stwx, tqx, hbmw) in enumerate(
                    ((wtg, stwg, tqg, gap), (wtu, stwu, tqu, uap))):
                tq = tqx[m % 2]
                if mmtok[wi][m % 2] is not None:
                    wt("vector", *mmtok[wi][m % 2])
                q = None
                for hf in range(2):
                    h = 2 * m + hf
                    stwx.wait("vector", h % 2)
                    em("vector", lambda e, b=wtb[h % 2], w=wi: e.tensor_scalar(
                        sgn[:], b[:], self.abc3[:, 3 + w:4 + w], None,
                        OP.is_lt))
                    q = em("vector", lambda e, b=wtb[h % 2], hh=hf, w=wi, t2=tq:
                           e.scalar_tensor_tensor(
                               t2[:, hh * 1024:(hh + 1) * 1024], b[:],
                               self.abc3[:, w:w + 1], sgn[:], OP.is_gt,
                               OP.subtract), inc="v")
                    if nld[wi] < 128:
                        hn = nld[wi]
                        stwx.load(h % 2, wtb[h % 2][:],
                                  hbmw[hn // 2, :,
                                       (hn % 2) * 1024:(hn % 2 + 1) * 1024],
                                  guard=("v", q))
                        nld[wi] += 1
                qtok[wi][m % 2] = ("v", q)

        quant_tile(0)
        vstk = [None, None]
        for m in range(65):
            if m < 64:
                for wi, tqx in enumerate((tqg, tqu)):
                    tq = tqx[m % 2]
                    wt("tensor", *qtok[wi][m % 2])
                    ev = evg if wi == 0 else evu
                    for ni, (n0, nsz) in enumerate(NTS):
                        if ev[ni] is not None:
                            wt("tensor", *ev[ni])
                        p = self.pb[wi + 2 * ni]
                        for j in range(16):
                            em("tensor", lambda e, pp=p, t2=tq, jj=j, nn0=n0,
                               nns=nsz, st=(j == 0), sp=(j == 15):
                               e.matmul(pp[:, 0:nns],
                                        t2[:, jj * 128:(jj + 1) * 128],
                                        xq[:, jj * T + nn0:jj * T + nn0 + nns],
                                        start=st, stop=sp),
                               inc=("t" if j == 15 else None))
                    mmtok[wi][m % 2] = ("t", Ss.now("t"))
            if m >= 1:
                mp = m - 1
                a2 = None
                for ni, (n0, nsz) in enumerate(NTS):
                    wt("tensor", *vstk[ni])
                    if evh[ni] is not None:
                        wt("tensor", *evh[ni])
                    em("tensor", lambda e, p=self.pb[4 + ni], s=stg[mp % 2],
                       nn0=n0, nns=nsz:
                       e.matmul(p[:, 0:nns], self.h128pm[:, 0:128],
                                s[:, nn0:nn0 + nns], start=True, stop=True),
                       inc="t")
                    wt("scalar", "t", Ss.now("t"))
                    a2 = em("scalar", lambda e, p=self.pb[4 + ni],
                            s=stg[mp % 2], nn0=n0, nns=nsz:
                            e.activation(s[:, nn0:nn0 + nns], p[:, 0:nns],
                                         AF.Identity), inc="a")
                    evh[ni] = ("a", a2)
                spl.load(mp % 2,
                         scA.rearrange("(q mm) t -> mm q t", mm=64)[mp],
                         stg[mp % 2][:], guard=("a", a2))
            if m < 64:
                tdone = Ss.now("t")
                if m + 1 < 64:
                    quant_tile(m + 1)
                for ni, (n0, nsz) in enumerate(NTS):
                    wt("vector", "t", tdone)
                    if ni == 0 and m >= 2:
                        spl.wait("vector", m % 2)
                    evg[ni] = ("v", em(
                        "vector", lambda e, p=self.pb[0 + 2 * ni],
                        nn0=n0, nns=nsz:
                        e.tensor_tensor(utmp[:, 0:nns], p[:, 0:nns],
                                        gsbc[:, nn0:nn0 + nns], OP.mult),
                        inc="v"))
                    wt("scalar", "v", Ss.now("v"))
                    a = em("scalar", lambda e, nns=nsz: e.activation(
                        stile[:, 0:nns], utmp[:, 0:nns], AF.Silu), inc="a")
                    wt("vector", "a", a)
                    evu[ni] = ("v", em(
                        "vector", lambda e, p=self.pb[1 + 2 * ni], nns=nsz:
                        e.tensor_tensor(utmp[:, 0:nns], stile[:, 0:nns],
                                        p[:, 0:nns], OP.mult), inc="v"))
                    vstk[ni] = ("v", em(
                        "vector", lambda e, s=stg[m % 2], nn0=n0, nns=nsz:
                        e.tensor_tensor(s[:, nn0:nn0 + nns], utmp[:, 0:nns],
                                        usbc[:, nn0:nn0 + nns], OP.mult),
                        inc="v"))
        for e2 in ("vector", "scalar", "tensor", "sync", "gpsimd"):
            spl.wait(e2, 0)
            spl.wait(e2, 1)
        Ss.barrier()
        stk_gu.close()
        sXQ.close()
        z8 = self.sb(sF, [128, 64 * T], BF16, "z8")
        stk_mx = ExitStack()
        runmx = self.sb(stk_mx, [128, T], F32)
        runmn = self.sb(stk_mx, [128, T], F32)
        em("vector", lambda e: e.memset(runmx[:], 0.0))
        em("vector", lambda e: e.memset(runmn[:], 0.0), inc="v")
        Ss.barrier(("tensor", "vector"))
        rel = Stream2(self, "sync", "rel", nslots=8)
        for k in range(8):
            rel.load(k, z8[:, k * T:(k + 1) * T],
                     scA[k * 128:(k + 1) * 128, :])
        ps_rel = [None, None]
        it = 0
        for k in range(64):
            rel.wait("tensor", k % 8)
            for ni, (n0, nsz) in enumerate(NTS):
                pi = it % 2
                ps = self.pb[pi]
                if ps_rel[pi] is not None:
                    wt("tensor", *ps_rel[pi])
                for par in range(2):
                    em("tensor", lambda e, p=ps, kk=k, pp=par, nn0=n0, nns=nsz:
                       e.matmul(p[pp * 64:pp * 64 + 64, 0:nns],
                                self.h64d[:, pp * 64:pp * 64 + 64],
                                z8[:, kk * T + nn0:kk * T + nn0 + nns],
                                start=True, stop=True),
                       inc=("t" if par == 1 else None))
                wt("vector", "t", Ss.now("t"))
                em("vector", lambda e, p=ps, nn0=n0, nns=nsz:
                   e.tensor_tensor(runmx[:, nn0:nn0 + nns],
                                   runmx[:, nn0:nn0 + nns],
                                   p[:, 0:nns], OP.max))
                cnt = em("vector", lambda e, p=ps, nn0=n0, nns=nsz:
                         e.tensor_tensor(runmn[:, nn0:nn0 + nns],
                                         runmn[:, nn0:nn0 + nns],
                                         p[:, 0:nns], OP.min), inc="v")
                ps_rel[pi] = ("v", cnt)
                it += 1
            if k + 8 < 64:
                rel.load((k + 8) % 8, z8[:, (k + 8) * T:(k + 9) * T],
                         scA[(k + 8) * 128:(k + 9) * 128, :],
                         guard=("t", Ss.now("t")))
        Ss.barrier()
        mrow = self.sb(stk_mx, [1, 3 * T], F32)
        self.token_max_row(runmx, runmn, mrow[0:1, 0:T],
                           mrow[0:1, 2 * T:3 * T], stk_mx)
        Ss.barrier(("vector",))
        rn = float(np.float32(np.sqrt(8192.0)))
        e2 = self.scrow[0:1, T:2 * T]
        sc2 = mrow[0:1, T:2 * T]
        tmpr = mrow[0:1, 2 * T:3 * T]
        em("vector", lambda e: e.tensor_scalar(e2, mrow[0:1, 0:T],
                                               self.c_invrn2, EPS,
                                               OP.mult, OP.max))
        em("vector", lambda e: e.tensor_single_scalar(tmpr, e2, rn, OP.mult))
        em("vector", lambda e: e.reciprocal(sc2, tmpr))
        em("vector", lambda e: e.tensor_single_scalar(sc2, sc2, 127.0, OP.mult))
        em("vector", lambda e: e.tensor_single_scalar(e2, e2, 1.0 / 127.0,
                                                      OP.mult))
        v = em("vector", lambda e: e.tensor_scalar(e2, e2,
                                                   self.wsc[0:1, 2:3], None,
                                                   OP.mult), inc="v")
        Ss.barrier()
        self.scbc2 = self.sb(self.stk, [128, T], F32, hi=True)
        self.ebc2 = self.sb(self.stk, [128, T], F32, hi=True)
        self.bcast_row(sc2, self.scbc2, self.pb[2], T)
        self.bcast_row(e2, self.ebc2, self.pb[2], T)
        Ss.barrier()
        stk_mx.close()
        return z8

    def p7(self, stk, xq2):
        Ss = self.S
        em, wt, dma = Ss.emit, Ss.wait, Ss.dma
        z8 = xq2
        scbc = self.scbc2
        lv0 = [self.sb(stk, [128, 512], F32) for _ in range(2)]
        lwt = [self.sb(stk, [128, 512], F32) for _ in range(2)]
        lct = [self.sb(stk, [128, 512], F32) for _ in range(2)]
        stkL = ExitStack()
        lx = [tuple(self.sb(stkL, [128, 512], F32) for _ in range(3))
              for _ in range(2)]
        wt_ = [self.sb(stk, [128, 1024], F32) for _ in range(4)]
        td = self.sb(stk, [128, 8192], BF16)
        sgn = self.sb(stk, [128, 1024], BF16)
        fdap = self.hbm["fdT"]
        h2ap = self.hbm["sch2"]
        st2f = Stream2(self, "scalar", "fd7", nslots=4)
        st2h = Stream2(self, "sync", "h27")
        st2o = Stream2(self, "sync", "o7")

        def fd_src(c):
            cc = c % 128
            return fdap[cc // 8, :, (cc % 8) * 1024:(cc % 8 + 1) * 1024]

        for c4 in range(4):
            st2f.load(c4, wt_[c4][:], fd_src(c4))
        qv = {}

        def fdq(p, m, tok_fn):
            toks = []
            for q8 in range(8):
                c = p * 128 + m * 8 + q8
                st2f.wait("vector", c % 4)
                if tok_fn is not None:
                    wt("vector", *tok_fn(q8))
                em("vector", lambda e, b=wt_[c % 4]: e.tensor_scalar(
                    sgn[:], b[:], self.abc3[:, 5:6], None, OP.is_lt))
                q = em("vector", lambda e, b=wt_[c % 4],
                       d=slice(q8 * 1024, (q8 + 1) * 1024):
                       e.scalar_tensor_tensor(td[:, d], b[:],
                                              self.abc3[:, 2:3], sgn[:],
                                              OP.is_gt, OP.subtract), inc="v")
                toks.append(q)
                if c + 4 < 256:
                    st2f.load((c + 4) % 4, wt_[(c + 4) % 4][:],
                              fd_src(c + 4), guard=("v", q))
            qv[(p, m)] = toks

        psfree = [None, None]
        pf = [None, None]

        def h64_pair(kA, n0):
            ptoks = []
            for k in (kA, kA + 1):
                pi = k % 2
                if psfree[pi] is not None:
                    wt("tensor", *psfree[pi])
                for par in range(2):
                    em("tensor", lambda e, p=self.pb[pi], kk=k, pp=par, nn0=n0:
                       e.matmul(p[pp * 64:pp * 64 + 64, 0:512],
                                self.h64d[:, pp * 64:pp * 64 + 64],
                                z8[:, kk * T + nn0:kk * T + nn0 + 512],
                                start=True, stop=True),
                       inc=("t" if par == 1 else None))
                ptoks.append(("t", Ss.now("t")))
            first, fin = self.ladder2_pair(
                [self.pb[k % 2][:, 0:512] for k in (kA, kA + 1)],
                [scbc[:, n0:n0 + 512]] * 2,
                [z8[:, k * T + n0:k * T + n0 + 512] for k in (kA, kA + 1)],
                [(lv0[k % 2], lwt[k % 2], lct[k % 2]) for k in (kA, kA + 1)],
                ptoks)
            for idx, k in enumerate((kA, kA + 1)):
                psfree[k % 2] = ("v", first[idx])
            return fin

        fdq(0, 0, None)
        psfree4 = [None] * 4
        bufs4 = [(lv0[0], lwt[0], lct[0]), (lv0[1], lwt[1], lct[1]),
                 lx[0], lx[1]]
        for kA in range(0, 64, 4):
            ptoks = []
            for k in range(kA, kA + 4):
                pi = k % 4
                if psfree4[pi] is not None:
                    wt("tensor", *psfree4[pi])
                for par in range(2):
                    em("tensor", lambda e, p=self.pb[pi], kk=k, pp=par:
                       e.matmul(p[pp * 64:pp * 64 + 64, 0:512],
                                self.h64d[:, pp * 64:pp * 64 + 64],
                                z8[:, kk * T:kk * T + 512],
                                start=True, stop=True),
                       inc=("t" if par == 1 else None))
                ptoks.append(("t", Ss.now("t")))
            first4, fin4 = self.ladder2_pair(
                [self.pb[k % 4][:, 0:512] for k in range(kA, kA + 4)],
                [scbc[:, 0:512]] * 4,
                [z8[:, k * T:k * T + 512] for k in range(kA, kA + 4)],
                bufs4, ptoks)
            for idx, k in enumerate(range(kA, kA + 4)):
                psfree4[k % 4] = ("v", first4[idx])
        ladfin = [Ss.now("v"), None]
        psfree[0] = psfree4[0]
        psfree[1] = psfree4[1]
        for e2 in ("vector", "scalar", "gpsimd", "sync"):
            Ss.wait(e2, "v", ladfin[0])
        stkL.close()
        h2t = [self.sb(stk, [128, 512], F32) for _ in range(2)]
        ot = [self.sb(stk, [128, 512], F32) for _ in range(2)]
        tmpf = self.sb(stk, [128, 512], F32)
        st2h.load(0, h2t[0][:], h2ap[0:128, 0:512])
        st2h.load(1, h2t[1][:], h2ap[128:256, 0:512])

        jt = {}
        for p, (n0, nsz) in enumerate(NTS):
            if p == 1:
                fdq(1, 0, lambda q8: ("t", jt[(0, 15, q8)]))
            for m in range(16):
                hs = p * 16 + m
                if m == 0:
                    wt("tensor", "v", ladfin[p])
                pb4 = self.pb[4 + m % 2]
                if pf[m % 2] is not None:
                    wt("tensor", *pf[m % 2])
                for j in range(64):
                    if j % 8 == 0:
                        wt("tensor", "v", qv[(p, m)][j // 8])
                    em("tensor", lambda e, pp=pb4, jj=j, nn0=n0:
                       e.matmul(pp[:, 0:512], td[:, jj * 128:(jj + 1) * 128],
                                z8[:, jj * T + nn0:jj * T + nn0 + 512],
                                start=(jj == 0), stop=(jj == 63)),
                       inc=("t" if j % 8 == 7 else None))
                    if j % 8 == 7:
                        jt[(p, m, j // 8)] = Ss.now("t")
                mmdone = Ss.now("t")
                if p == 0 and m + 1 < 16:
                    fdq(0, m + 1, lambda q8, mm=m: ("t", jt[(0, mm, q8)]))
                elif p == 1 and m + 1 < 16:
                    fdq(1, m + 1, lambda q8, mm=m: ("t", jt[(1, mm, q8)]))
                if p == 0:
                    h64_pair(4 * m, 512)
                    h64_pair(4 * m + 2, 512)
                    if m == 15:
                        ladfin[1] = Ss.now("v")
                st2h.wait("vector", hs % 2)
                if hs >= 2:
                    st2o.wait("vector", hs % 2)
                wt("vector", "t", mmdone)
                tk = em("vector", lambda e, pp=pb4, nn0=n0: e.tensor_tensor(
                    tmpf[:], pp[:, 0:512], self.ebc2[:, nn0:nn0 + 512],
                    OP.mult), inc="v")
                pf[m % 2] = ("v", tk)
                cnt = em("vector", lambda e, oo=ot[hs % 2], hh2=h2t[hs % 2]:
                         e.tensor_tensor(oo[:], tmpf[:], hh2[:], OP.add),
                         inc="v")
                wt("sync", "v", cnt)
                st2o.load(hs % 2,
                          self.hbm["outT"][m * 128:(m + 1) * 128,
                                           n0:n0 + 512], ot[hs % 2][:])
                if hs + 2 < 32:
                    p2, m2 = divmod(hs + 2, 16)
                    n02 = NTS[p2][0]
                    st2h.load(hs % 2, h2t[hs % 2][:],
                              h2ap[m2 * 128:(m2 + 1) * 128, n02:n02 + 512],
                              guard=("v", cnt))
        st2o.wait("sync", 0)
        st2o.wait("sync", 1)
        Ss.barrier()

    def _phase_gate(self, n):
        if n >= PHASE_LIMIT:
            Ss = self.S
            Ss.barrier()
            Ss.dma("sync", self.hbm["outT"][0:128, :],
                   self.identf[:, 0:1] if False else self.arena.t[0:128, 0:2048].bitcast(F32)[:, 0:1024], "st")
            Ss.wait("sync", "st", None)
            Ss.barrier()
            return True
        return False

    def emit_all(self):
        Ss = self.S
        sP0 = ExitStack()
        self.p0(sP0)
        sP0.close()
        Ss.barrier()
        if self._phase_gate(1):
            return
        sB = ExitStack()
        self.siluz = self.sb(sB, [128, 16 * T], BF16, "siluz")
        sROWS = ExitStack()
        self.rowsA = self.sb(sROWS, [32, 4 * T], F32, "rowsA")
        self.Hn = self.sb(sROWS, [64, 2048], F32, "Hn")
        self.Hn2b = self.sb(sROWS, [128, 2048], BF16, "Hn2b")
        sSTAT = ExitStack()
        Ss.emit("vector", lambda e: e.memset(self.Hn[:], 0.0), inc="v")
        Ss.barrier()
        sA = ExitStack()
        self.xbc = self.sb(sA, [128, 24 * T], BF16, "xbc")
        sR = ExitStack()
        self.rm1 = self.sb(sR, [128, 16 * TN], BF16, "rm1")
        self.rowsB = self.sb(sR, [32, 4 * T], F32, "rowsB")
        with ExitStack() as sp:
            statp = self.sb(sp, [1, 3 * TN], F32)
            self.stage1("xprevT", sp, full=False, stat=statp)
        with ExitStack() as s:
            self.scan(s, state_only=True)
        Ss.barrier()
        if self._phase_gate(3):
            return
        self.stat_main = self.sb(sSTAT, [1, 3 * TN], F32, "statm", hi=True)
        self.stage1("xT", sA, full=True, stat=self.stat_main)
        sR.close()
        if self._phase_gate(4):
            return
        self.ya = self.sb(sB, [128, 16 * T], BF16, "ya")
        with ExitStack() as s:
            self.scan(s, state_only=False)
        Ss.barrier()
        sA.close()
        sROWS.close()
        if self._phase_gate(5):
            return
        sC = ExitStack()
        self.r2a = self.sb(sC, [128, 16 * T], F32, "r2a", hi=True)
        with ExitStack() as s:
            self.p3(s)
        sB.close()
        sSTAT.close()
        Ss.barrier()
        if self._phase_gate(6):
            return
        sD = ExitStack()
        self.rms2 = self.sb(sD, [128, 16 * T], BF16, "rms2", hi=True)
        with ExitStack() as s:
            self.p4(s)
        sD.close()
        Ss.barrier()
        if self._phase_gate(7):
            return
        sXQ = ExitStack()
        with ExitStack() as s:
            self.p5(s, sC, sXQ)
        Ss.barrier()
        if self._phase_gate(8):
            return
        sF = ExitStack()
        with ExitStack() as s:
            z8 = self.p6(s, sF, sXQ)
        Ss.barrier()
        self.sSCROW.close()
        if self._phase_gate(9):
            return
        with ExitStack() as s:
            self.p7(s, z8)
        sF.close()

    def finalize(self):
        with self.nc.Block() as block:
            self.S.build(block)
        self.stk.close()
        return self.nc


_HOST_CACHE = {}


def _tile_w(Wo):
    M, K = Wo.shape
    mb, kb = M // 128, K // 128
    return np.ascontiguousarray(
        Wo.reshape(mb, 128, kb, 128).transpose(0, 3, 2, 1).reshape(mb, 128, K),
        dtype=np.float32)


def _host_prepare(inputs):
    f32 = np.float32
    win = np.asarray(inputs["L0_attn_in_proj_w"], f32)
    winp = np.zeros((41 * 128, 2048), f32)
    winp[0:5152] = win
    consts = {
        "winT": _tile_w(winp),
        "woutT": _tile_w(np.asarray(inputs["L0_attn_out_proj_w"], f32)),
        "wgT": _tile_w(np.asarray(inputs["L0_attn_gate_w"], f32)),
        "wuT": _tile_w(np.asarray(inputs["L0_attn_up_w"], f32)),
        "wdT": _tile_w(np.asarray(inputs["L0_attn_down_w"], f32)),
        "fgT": _tile_w(np.asarray(inputs["L0_ffn_w_gate"], f32)),
        "fuT": _tile_w(np.asarray(inputs["L0_ffn_w_up"], f32)),
    }
    fd = np.asarray(inputs["L0_ffn_w_down"], f32)
    kidx = np.empty(8192, np.int64)
    for j in range(64):
        for kp in range(128):
            kidx[j * 128 + kp] = (kp % 64) * 128 + 2 * j + kp // 64
    consts["fdT"] = _tile_w(np.ascontiguousarray(fd[:, kidx]))
    consts["convw"] = np.ascontiguousarray(
        np.asarray(inputs["L0_attn_conv_w"], f32).reshape(24, 128, 4)
        .transpose(1, 0, 2).reshape(128, 96))
    consts["convb"] = np.ascontiguousarray(
        np.asarray(inputs["L0_attn_conv_b"], f32).reshape(24, 128).T)
    consts["dtb"] = np.asarray(inputs["L0_attn_dt_bias"], f32).reshape(32, 1)
    consts["alog"] = np.asarray(inputs["L0_attn_A_log"], f32).reshape(32, 1)
    for nm, key in (("wln", "L0_attn_input_ln_w"), ("wpost", "L0_attn_post_ln_w"),
                    ("ssmw", "L0_attn_ssm_norm_w")):
        consts[nm] = np.ascontiguousarray(
            np.asarray(inputs[key], f32).reshape(16, 128).T)
    Dh = np.asarray(inputs["L0_attn_D"], f32)
    dd = np.zeros((128, 2048), f32)
    for h in range(32):
        g = h // 4
        b = (g % 2) * 64
        for p in range(64):
            dd[b + p, h * 64 + p] = Dh[h]
    consts["ddiag"] = dd
    H128 = _hadamard(128)
    consts["h128pm"] = np.ascontiguousarray(np.hstack([H128, -H128]))
    H64 = _hadamard(64)
    h64z = np.zeros((128, 128), np.float32)
    h64z[0:64, 0:64] = H64
    h64z[64:128, 64:128] = H64
    consts["h64d"] = h64z
    tri = (np.arange(Q)[:, None] <= np.arange(Q)[None, :]).astype(f32)
    consts["tri4"] = np.ascontiguousarray(np.tile(tri, (1, 4)))
    consts["identf"] = np.eye(128, dtype=f32)
    rowc = np.zeros((1, 448), f32)
    rowc[0, 0:128] = 1.0
    rowc[0, 128:192] = 1.0
    rowc[0, 320:384] = 1.0
    rowc[0, 384] = np.float32(1.0) / np.float32(np.sqrt(2048.0))
    rowc[0, 385] = np.float32(1.0) / np.float32(np.sqrt(8192.0))
    rowc[0, 386] = np.float32(1.0 / (8192.0 * 2048.0))
    consts["rowc"] = rowc
    consts["onescol"] = np.ones((128, 1), f32)
    lt = np.zeros((128, 24), f32)
    for j, kk in enumerate(LAD_KS):
        lt[:, j] = np.float32(kk) - np.float32(128.5)
        lt[:, 8 + j] = LAD_THR[j]
        lt[:, 16 + j] = -LAD_THR[j]
    consts["ladthr"] = lt
    return consts


def kernel(**inputs):
    consts = _host_prepare(inputs)
    x = np.asarray(inputs["L0_attn_hidden_states"], np.float32)
    in_maps = []
    for core in range(8):
        b, half = core // 2, core % 2
        t0 = half * T
        xs = np.ascontiguousarray(x[b].T)
        xT = np.zeros((D, TN), np.float32)
        xT[:, HALO:] = xs[:, t0:t0 + T]
        if half == 1:
            xT[:, :HALO] = xs[:, t0 - HALO:t0]
        xprevT = np.zeros((D, TN), np.float32)
        if half == 1:
            xprevT[:, HALO:] = xs[:, 0:T]
        m = dict(consts)
        m["xT"] = xT
        m["xprevT"] = xprevT
        in_maps.append(m)
    P = Prog()
    P.emit_all()
    nc = P.finalize()
    res = run_bass_kernel_spmd(nc, in_maps, core_ids=list(range(8)))
    out = np.empty((4, 2048, 2048), np.float32)
    for core in range(8):
        b, half = core // 2, core % 2
        t0 = half * T
        out[b, t0:t0 + T, :] = res.results[core]["outT"].T
    return out


# version 19
# speedup vs baseline: 1.0313x; 1.0313x over previous
from contextlib import ExitStack

import numpy as np
import ml_dtypes

import concourse.bass as bass
import concourse.mybir as mybir
from concourse.bass_utils import run_bass_kernel_spmd
from concourse.alu_op_type import AluOpType as OP

F32 = mybir.dt.float32
BF16 = mybir.dt.bfloat16
AF = mybir.ActivationFunctionType
BFNP = ml_dtypes.bfloat16

D = 2048
NH = 32
DFF = 8192
T = 1024
HALO = 3
TN = T + HALO
Q = 128
NCH = 8
EPS = 1e-5
CLAMP = 30.0

_DEBUG = []
PHASE_LIMIT = 99
SCAN_PART = 99

ENGS = ("sync", "gpsimd", "vector", "scalar", "tensor")
NTS = ((0, 512), (512, 512))
NTS_H = ((0, 512), (512, 512), (1024, 3))


LAD_KS = (128, 64, 32, 16, 8, 4, 2, 1)
LAD_ACT = (1, 3, 5, 6, 7)


def _lad_thr():
    d = np.float32(0.0)
    thr = []
    for j, k in enumerate(LAD_KS):
        thr.append(np.float32(np.float32(k) - np.float32(128.5) + d))
        if j in LAD_ACT:
            d = np.float32(d + np.float32(k) * np.float32(0.5))
    return thr, d


LAD_THR, LAD_D = _lad_thr()
LAD_FC = float(np.float32(128.0) - LAD_D)


def _hadamard(n):
    H = np.array([[1.0]], dtype=np.float64)
    while H.shape[0] < n:
        H = np.block([[H, H], [H, -H]])
    return H.astype(np.float32)


class S:

    def __init__(self, nc, stk):
        self.nc = nc
        self.stk = stk
        self.q = {e: [] for e in ENGS}
        self.sems = {}
        self.cnt = {}

    def sem(self, name):
        if name not in self.sems:
            self.sems[name] = self.stk.enter_context(self.nc.semaphore(name))
            self.cnt[name] = 0
        return self.sems[name]

    def emit(self, eng, fn, inc=None, amt=1):
        dr = eng in ("vector", "scalar", "gpsimd")
        if inc is not None:
            self.sem(inc)
            self.cnt[inc] += amt
            h = self.sems[inc]
            if dr:
                def g(e, f=fn, hh=h, a=amt):
                    f(e).then_inc(hh, a)
                    e.drain()
                self.q[eng].append(g)
            else:
                self.q[eng].append(
                    lambda e, f=fn, hh=h, a=amt: f(e).then_inc(hh, a))
        else:
            if dr:
                def g(e, f=fn):
                    f(e)
                    e.drain()
                self.q[eng].append(g)
            else:
                self.q[eng].append(fn)
        return self.cnt.get(inc)

    def dma(self, eng, out, in_, inc):
        return self.emit(eng, lambda e, o=out, i=in_: e.dma_start(out=o, in_=i),
                         inc=inc, amt=16)

    def now(self, sem):
        return self.cnt.get(sem, 0)

    def wait(self, eng, sem, val=None):
        self.sem(sem)
        v = self.cnt[sem] if val is None else val
        h = self.sems[sem]
        self.q[eng].append(lambda e, hh=h, vv=v: e.wait_ge(hh, vv))

    def barrier(self, engines=ENGS):
        snap = [(n, self.cnt[n]) for n in self.sems if self.cnt[n] > 0]
        for eng in engines:
            for n, v in snap:
                self.wait(eng, n, v)

    def build(self, block):
        for ename in ENGS:
            ops = self.q[ename]
            dec = getattr(block, ename)

            def _body(eng, _ops=ops):
                for fn in _ops:
                    fn(eng)
            dec(_body)


class Stream2:

    def __init__(self, P, eng, tag, nslots=2):
        self.P, self.eng = P, eng
        self.names = [f"{tag}{i}" for i in range(nslots)]
        for n in self.names:
            P.S.sem(n)
        self.tok = [P.S.now(n) for n in self.names]

    def load(self, slot, out, in_, guard=None):
        if guard is not None:
            gl = guard if isinstance(guard, list) else [guard]
            for g in gl:
                self.P.S.wait(self.eng, *g)
        self.P.S.dma(self.eng, out, in_, self.names[slot])
        self.tok[slot] = self.P.S.now(self.names[slot])

    def wait(self, eng, slot):
        self.P.S.wait(eng, self.names[slot], self.tok[slot])


class Arena:

    def __init__(self, nc, stk):
        self.t = None
        for trial, kb in enumerate(range(220, 120, -2)):
            try:
                self.t = stk.enter_context(
                    nc.sbuf_tensor(f"arena{trial}", [128, kb * 512], BF16))
                self.size = kb * 1024
                break
            except AssertionError:
                continue
        assert self.t is not None, "arena allocation failed"
        self.free_list = [(0, self.size)]

    def alloc(self, nbytes, hi=False):
        nbytes = (nbytes + 63) & ~63
        idxs = range(len(self.free_list) - 1, -1, -1) if hi \
            else range(len(self.free_list))
        for i in idxs:
            o, s = self.free_list[i]
            if s >= nbytes:
                if s == nbytes:
                    self.free_list.pop(i)
                    return o
                if hi:
                    self.free_list[i] = (o, s - nbytes)
                    return o + s - nbytes
                self.free_list[i] = (o + nbytes, s - nbytes)
                return o
        raise AssertionError(
            f"arena OOM: need {nbytes}, free={self.free_list}")

    def free(self, off, nbytes):
        nbytes = (nbytes + 63) & ~63
        self.free_list.append((off, nbytes))
        self.free_list.sort()
        merged = []
        for o, s in self.free_list:
            if merged and merged[-1][0] + merged[-1][1] == o:
                merged[-1] = (merged[-1][0], merged[-1][1] + s)
            else:
                assert not merged or merged[-1][0] + merged[-1][1] <= o, \
                    f"arena double-free/overlap at {off}: {self.free_list}"
                merged.append([o, s])
        self.free_list = [tuple(x) for x in merged]


class Prog:
    def __init__(self):
        self.nc = bass.Bass("TRN2", target_bir_lowering=False)
        self.stk = ExitStack()
        self.S = S(self.nc, self.stk)
        self.hbm = {}
        self._declare_hbm()
        E = self.stk.enter_context
        self.pb = [E(self.nc.psum_tensor(f"pb{i}", [128, 512], F32))
                   for i in range(7)]
        self.pbT = E(self.nc.psum_tensor("pbT", [128, 1024], BF16))
        self._alloc_const()
        self.arena = Arena(self.nc, self.stk)
        self._uid = 0

    def uid(self):
        self._uid += 1
        return self._uid

    def sb(self, stk, shape, dt, name=None, hi=False):
        p, n = shape
        esz = 2 if dt == BF16 else 4
        nbytes = n * esz
        off = self.arena.alloc(nbytes, hi=hi)
        if stk is not None:
            stk.callback(self.arena.free, off, nbytes)
        view = self.t_arena()[0:p, off // 2:off // 2 + (nbytes + 1) // 2]
        return view if dt == BF16 else view.bitcast(F32)

    def t_arena(self):
        return self.arena.t

    def _declare_hbm(self):
        nc = self.nc

        def add(n, shp, dt=F32, kind="ExternalInput"):
            self.hbm[n] = nc.dram_tensor(n, shp, dt, kind=kind).ap()
        add("xT", [D, TN])
        add("xprevT", [D, TN])
        add("winT", [41, 128, 2048])
        add("woutT", [16, 128, 2048])
        add("wgT", [64, 128, 2048])
        add("wuT", [64, 128, 2048])
        add("wdT", [16, 128, 8192])
        add("fgT", [64, 128, 2048])
        add("fuT", [64, 128, 2048])
        add("fdT", [16, 128, 8192])
        add("convw", [128, 96])
        add("convb", [128, 24])
        add("dtb", [32, 1])
        add("alog", [32, 1])
        add("wln", [128, 16])
        add("wpost", [128, 16])
        add("ssmw", [128, 16])
        add("ddiag", [128, 2048])
        add("h128pm", [128, 256])
        add("h64d", [128, 128])
        add("tri4", [128, 512])
        add("identf", [128, 128])
        add("rowc", [1, 448])
        add("ladthr", [128, 24])
        add("onescol", [128, 1])
        add("scA", [DFF, T], BF16, "Internal")
        add("sch2", [D, T], F32, "Internal")
        add("outT", [D, T], F32, "ExternalOutput")
        for n, shp, dt in _DEBUG:
            add(n, shp, dt, "ExternalOutput")

    def dump(self, name, ap, eng="sync"):
        if name in self.hbm:
            self.S.barrier()
            self.S.dma(eng, self.hbm[name], ap, "st")
            self.S.barrier()

    def _alloc_const(self):
        nc, Ss, E = self.nc, self.S, self.stk.enter_context
        self.identf = E(nc.sbuf_tensor("c_identf", [128, 128], F32))
        self.identb = E(nc.sbuf_tensor("c_identb", [128, 128], BF16))
        self.h128pm = E(nc.sbuf_tensor("c_h128pm", [128, 256], BF16))
        self.h64d = E(nc.sbuf_tensor("c_h64d", [128, 128], BF16))
        self.tri4 = E(nc.sbuf_tensor("c_tri4", [128, 512], BF16))
        self.convw = E(nc.sbuf_tensor("c_convw", [128, 96], F32))
        self.convb = E(nc.sbuf_tensor("c_convb", [128, 24], F32))
        self.dtb = E(nc.sbuf_tensor("c_dtb", [32, 1], F32))
        self.alog = E(nc.sbuf_tensor("c_alog", [32, 1], F32))
        self.aexp = E(nc.sbuf_tensor("c_aexp", [32, 1], F32))
        self.wln = E(nc.sbuf_tensor("c_wln", [128, 16], F32))
        self.wpost = E(nc.sbuf_tensor("c_wpost", [128, 16], F32))
        self.ssmw = E(nc.sbuf_tensor("c_ssmw", [128, 16], F32))
        self.ddiag = E(nc.sbuf_tensor("c_ddiag", [128, 2048], BF16))
        self.rowc = E(nc.sbuf_tensor("c_rowc", [1, 448], F32))
        self.ladthr = E(nc.sbuf_tensor("c_ladthr", [128, 24], F32))
        self.onescol = E(nc.sbuf_tensor("c_onescol", [128, 1], F32))
        self.zeros32 = E(nc.sbuf_tensor("c_zeros32", [32, Q], F32))
        h = self.hbm
        for dst, src in [(self.identf, h["identf"]), (self.convw, h["convw"]),
                         (self.convb, h["convb"]), (self.dtb, h["dtb"]),
                         (self.alog, h["alog"]), (self.wln, h["wln"]),
                         (self.wpost, h["wpost"]), (self.ssmw, h["ssmw"]),
                         (self.rowc, h["rowc"]), (self.onescol, h["onescol"]),
                         (self.ladthr, h["ladthr"])]:
            Ss.dma("sync", dst[:], src, "win")
        for dst, src in [(self.identb, h["identf"]), (self.h128pm, h["h128pm"]),
                         (self.h64d, h["h64d"]), (self.tri4, h["tri4"]),
                         (self.ddiag, h["ddiag"])]:
            Ss.dma("gpsimd", dst[:], src, "gpd")
        Ss.emit("vector", lambda e: e.memset(self.zeros32[:], 0.0))
        Ss.wait("scalar", "win", None)
        Ss.emit("scalar", lambda e: e.activation(self.aexp[:], self.alog[:], AF.Exp),
                inc="a")
        self.ones128 = self.rowc[0:1, 0:128]
        self.top64 = self.rowc[0:1, 128:256]
        self.bot64 = self.rowc[0:1, 256:384]
        self.c_invrn1 = self.rowc[0:1, 384:385]
        self.c_invrn2 = self.rowc[0:1, 385:386]
        self.c_nel = self.rowc[0:1, 386:387]

    def bcast_row(self, row_ap, dst_ap, psum, n):
        Ss = self.S
        Ss.wait("tensor", "v", Ss.now("v"))
        Ss.wait("tensor", "a", Ss.now("a"))
        last = None
        for c0 in range(0, n, 512):
            cs = min(512, n - c0)
            if last is not None:
                Ss.wait("tensor", *last)
            Ss.emit("tensor", lambda e, a=row_ap, d=psum, c=c0, s=cs:
                    e.matmul(d[:, 0:s], self.ones128, a[0:1, c:c + s],
                             start=True, stop=True), inc="t")
            tc = Ss.now("t")
            Ss.wait("vector", "t", tc)
            cnt = Ss.emit("vector", lambda e, d=dst_ap, p=psum, c=c0, s=cs:
                          e.tensor_copy(d[:, c:c + s], p[:, 0:s]), inc="v")
            last = ("v", cnt)
        return last

    def stream_mm(self, wname, m_list, rhs_fn, evac_fn, ps_pair, wbufs, kb,
                  kcol0=0, nts=NTS, mrows_fn=None):
        Ss = self.S
        wap = self.hbm[wname]
        st2 = Stream2(self, "gpsimd", f"wm{self.uid()}_")
        buf_use = [0, 0]

        def load(mi):
            m = m_list[mi]
            st2.load(mi % 2, wbufs[mi % 2][:, 0:kb * 128],
                     wap[m, :, kcol0 * 128:(kcol0 + kb) * 128],
                     guard=(("t", buf_use[mi % 2]) if buf_use[mi % 2] else None))
        load(0)
        if len(m_list) > 1:
            load(1)
        ps_rel = [None, None]
        it = 0
        for mi, m in enumerate(m_list):
            mr = mrows_fn(m) if mrows_fn else 128
            for ni, (n0, nsz) in enumerate(nts):
                pi = it % 2
                ps = ps_pair[pi]
                if ps_rel[pi] is not None:
                    Ss.wait("tensor", *ps_rel[pi])
                if ni == 0:
                    st2.wait("tensor", mi % 2)
                for j in range(kb):
                    Ss.emit("tensor",
                            lambda e, pj=ps, w=wbufs[mi % 2], jj=j, nn0=n0,
                            nns=nsz, st=(j == 0), sp=(j == kb - 1), mm=mr:
                            e.matmul(pj[0:mm, 0:nns],
                                     w[:, jj * 128:jj * 128 + mm],
                                     rhs_fn(jj, nn0, nns),
                                     start=st, stop=sp),
                            inc=("t" if j == kb - 1 else None))
                tdone = Ss.now("t")
                buf_use[mi % 2] = tdone
                ps_rel[pi] = evac_fn(m, ni, n0, nsz, ps, tdone)
                it += 1
            if mi + 2 < len(m_list):
                load(mi + 2)

    def ln_stats(self, src_hbm, ncols, stat, stk, with_rho):
        Ss = self.S
        xb = [self.sb(stk, [128, ncols], F32) for _ in range(2)]
        sq = self.sb(stk, [128, ncols], F32)
        scr = self.sb(stk, [1, 2 * ncols], F32)
        epsc = self.sb(stk, [1, 1], F32)
        Ss.emit("vector", lambda e: e.memset(epsc[:], EPS), inc="v")
        segs = [(0, 512), (512, 512)]
        if ncols > 1024:
            segs.append((1024, ncols - 1024))
        ps_s = [self.pb[1], self.pb[2], self.pb[3]]
        ps_q = [self.pb[4], self.pb[5], self.pb[6]]
        st2 = Stream2(self, "sync", f"ln{self.uid()}_")
        st2.load(0, xb[0][:], src_hbm[0:128, 0:ncols])
        st2.load(1, xb[1][:], src_hbm[128:256, 0:ncols])
        tprev = 0
        for c in range(16):
            st2.wait("scalar", c % 2)
            if c >= 1:
                Ss.wait("scalar", "t", tprev)
            acnt = Ss.emit("scalar", lambda e, b=xb[c % 2]:
                           e.activation(sq[:], b[:], AF.Square), inc="a")
            Ss.wait("tensor", "a", acnt)
            st2.wait("tensor", c % 2)
            for si, (s0, sz) in enumerate(segs):
                for psm, srcb in ((ps_s[si], xb[c % 2]), (ps_q[si], sq)):
                    Ss.emit("tensor", lambda e, p=psm, b=srcb, s0_=s0, sz_=sz,
                            st=(c == 0), sp=(c == 15):
                            e.matmul(p[0:1, 0:sz_], self.onescol[:, 0:1],
                                     b[:, s0_:s0_ + sz_], start=st, stop=sp),
                            inc="t")
            tprev = Ss.now("t")
            if c + 2 < 16:
                st2.load(c % 2, xb[c % 2][:],
                         src_hbm[(c + 2) * 128:(c + 3) * 128, 0:ncols],
                         guard=[("t", tprev), ("a", acnt)])
        Ss.wait("vector", "t", tprev)
        for si, (s0, sz) in enumerate(segs):
            Ss.emit("vector", lambda e, p=ps_s[si], s0_=s0, sz_=sz:
                    e.tensor_copy(scr[0:1, s0_:s0_ + sz_], p[0:1, 0:sz_]))
            Ss.emit("vector", lambda e, p=ps_q[si], s0_=s0, sz_=sz:
                    e.tensor_copy(scr[0:1, ncols + s0_:ncols + s0_ + sz_],
                                  p[0:1, 0:sz_]), inc="v")
        N = ncols
        mean = stat[0:1, 0:N]
        rr_ = stat[0:1, N:2 * N]
        rrho = stat[0:1, 2 * N:3 * N]
        s1 = scr[0:1, 0:N]
        s2 = scr[0:1, N:2 * N]
        em, wt = Ss.emit, Ss.wait
        inv_d = 1.0 / 2048.0
        em("vector", lambda e: e.tensor_single_scalar(mean, s1, inv_d, OP.mult))
        em("vector", lambda e: e.tensor_single_scalar(s2, s2, inv_d, OP.mult))
        em("vector", lambda e: e.tensor_tensor(s1, mean, mean, OP.mult))
        v = em("vector", lambda e: e.tensor_tensor(s2, s2, s1, OP.subtract),
               inc="v")
        wt("scalar", "v", v)
        a = em("scalar", lambda e: e.activation(s1, s2, AF.Sqrt,
                                                bias=epsc[0:1, 0:1]), inc="a")
        wt("vector", "a", a)
        v = em("vector", lambda e: e.reciprocal(rr_, s1), inc="v")
        if with_rho:
            em("vector", lambda e: e.tensor_tensor(s1, rr_, rr_, OP.mult))
            v = em("vector", lambda e: e.tensor_tensor(s1, s1, s2, OP.mult),
                   inc="v")
            wt("scalar", "v", v)
            a = em("scalar", lambda e: e.activation(s1, s1, AF.Sqrt,
                                                    bias=epsc[0:1, 0:1]), inc="a")
            wt("vector", "a", a)
            em("vector", lambda e: e.reciprocal(s1, s1) if False else
               e.reciprocal(s2, s1))
            v = em("vector", lambda e: e.tensor_tensor(rrho, rr_, s2, OP.mult),
                   inc="v")
        return ("v", v)

    def prescale(self, src_hbm, ncols, mbc, rrbc, wcol, dst, dst_stride, stk,
                 tag):
        Ss = self.S
        xb = [self.sb(stk, [128, ncols], F32) for _ in range(2)]
        tmp = self.sb(stk, [128, ncols], F32)
        st2 = Stream2(self, "sync", f"pr{self.uid()}_")
        st2.load(0, xb[0][:], src_hbm[0:128, 0:ncols])
        st2.load(1, xb[1][:], src_hbm[128:256, 0:ncols])
        aprev = 0
        vprev = 0
        for c in range(16):
            st2.wait("vector", c % 2)
            if wcol is not None and c >= 1:
                Ss.wait("vector", "a", aprev)
            Ss.emit("vector", lambda e, b=xb[c % 2]:
                    e.tensor_tensor(tmp[:], b[:], mbc[:, 0:ncols], OP.subtract))
            if wcol is None:
                vcnt = Ss.emit("vector", lambda e, cc=c:
                               e.tensor_tensor(dst[:, cc * dst_stride:cc * dst_stride + ncols],
                                               tmp[:], rrbc[:, 0:ncols], OP.mult),
                               inc="v")
            else:
                vcnt = Ss.emit("vector", lambda e:
                               e.tensor_tensor(tmp[:], tmp[:], rrbc[:, 0:ncols], OP.mult),
                               inc="v")
                Ss.wait("scalar", "v", vcnt)
                aprev = Ss.emit("scalar", lambda e, cc=c:
                        e.activation(dst[:, cc * dst_stride:cc * dst_stride + ncols],
                                     tmp[:], AF.Identity, scale=wcol[:, cc:cc + 1]),
                        inc="a")
            vprev = Ss.now("v")
            if c + 2 < 16:
                st2.load(c % 2, xb[c % 2][:],
                         src_hbm[(c + 2) * 128:(c + 3) * 128, 0:ncols],
                         guard=("v", vprev))

    def crop_store(self, n0, nsz):
        if n0 == 0:
            return 3, 0, nsz - 3
        return 0, n0 - 3, nsz

    def conv_chunk(self, c, rb, stk):
        Ss = self.S
        Ss.wait("scalar", "v", Ss.now("v"))
        toks = []
        for hi, h0 in enumerate((0, 512)):
            psm = self.accq[:, hi * 512:(hi + 1) * 512]
            acnt = Ss.emit("scalar", lambda e, p=psm, b=rb, hh=h0, cc=c:
                           e.activation(p, b[:, hh:hh + 512],
                                        AF.Identity,
                                        scale=self.convw[:, cc * 4:cc * 4 + 1],
                                        bias=self.convb[:, cc:cc + 1]), inc="a")
            Ss.wait("vector", "a", acnt)
            vcnt = None
            for k in (1, 2, 3):
                vcnt = Ss.emit("vector", lambda e, p=psm, b=rb, hh=h0, cc=c, kk=k:
                               e.scalar_tensor_tensor(
                                   p, b[:, hh + kk:hh + kk + 512],
                                   self.convw[:, cc * 4 + kk:cc * 4 + kk + 1],
                                   p, OP.mult, OP.add), inc="v")
            Ss.wait("scalar", "v", vcnt)
            acnt = Ss.emit("scalar", lambda e, p=psm, hh=h0, cc=c:
                           e.activation(self.xbc[:, cc * T + hh:cc * T + hh + 512],
                                        p, AF.Silu), inc="a")
            toks.append(("a", acnt))
        return toks[-1]

    def dt_rows(self, stk):
        Ss = self.S
        rA = lambda i: self.rowsA[0:32, i * T:(i + 1) * T]
        rB = lambda i: self.rowsB[0:32, i * T:(i + 1) * T]
        dtraw, dtv, L, expL = rB(0), rB(1), rA(0), rA(1)
        wgt, ldtmL, scr, scr2 = rA(2), rA(3), rB(2), rB(3)
        rsl = lambda i, a, b: (self.rowsA if i < 4 else None, )
        em, wt = Ss.emit, Ss.wait
        em("vector", lambda e: e.tensor_scalar(scr, dtraw, self.dtb[:, 0:1],
                                               None, OP.add))
        v = em("vector", lambda e: e.tensor_copy(scr2, scr), inc="v")
        wt("scalar", "v", v)
        a = em("scalar", lambda e: e.activation(scr2, scr2, AF.Abs), inc="a")
        a = em("scalar", lambda e: e.activation(scr2, scr2, AF.Exp, scale=-1.0),
               inc="a")
        a = em("scalar", lambda e: e.activation(scr2, scr2, AF.Ln, bias=1.0),
               inc="a")
        wt("vector", "a", a)
        em("vector", lambda e: e.tensor_single_scalar(scr, scr, 0.0, OP.max))
        v = em("vector", lambda e: e.tensor_tensor(dtv, scr, scr2, OP.add),
               inc="v")
        em("vector", lambda e: e.tensor_scalar(scr, dtv, self.aexp[:, 0:1],
                                               -1.0, OP.mult, OP.mult))
        for i in range(NCH):
            v = em("vector", lambda e, ii=i: e.tensor_tensor_scan(
                self.rowsA[0:32, ii * Q:(ii + 1) * Q],
                self.rowsB[0:32, 2 * T + ii * Q:2 * T + (ii + 1) * Q],
                self.zeros32[:], 0.0, OP.add, OP.add), inc="v")
        wt("scalar", "v", v)
        a = em("scalar", lambda e: e.activation(expL, L, AF.Exp), inc="a")
        v = em("vector", lambda e: e.tensor_single_scalar(scr2, dtv, 1e-38,
                                                          OP.max), inc="v")
        wt("scalar", "v", v)
        a = em("scalar", lambda e: e.activation(scr2, scr2, AF.Ln), inc="a")
        wt("vector", "a", a)
        em("vector", lambda e: e.tensor_tensor(ldtmL, scr2, L, OP.subtract))
        for i in range(NCH):
            v = em("vector", lambda e, ii=i: e.tensor_scalar(
                self.rowsB[0:32, 2 * T + ii * Q:2 * T + (ii + 1) * Q],
                self.rowsA[0:32, ii * Q:(ii + 1) * Q],
                self.rowsA[0:32, ii * Q + Q - 1:ii * Q + Q],
                -1.0, OP.subtract, OP.mult), inc="v")
        wt("scalar", "v", v)
        a = em("scalar", lambda e: e.activation(scr2, scr, AF.Exp), inc="a")
        wt("vector", "a", a)
        v = em("vector", lambda e: e.tensor_tensor(wgt, dtv, scr2, OP.mult),
               inc="v")
        return ("v", v)

    def stage1(self, src_name, stk_outer, full, stat):
        Ss = self.S
        src = self.hbm[src_name]
        stk = ExitStack()
        with ExitStack() as lstk:
            self.ln_stats(src, TN, stat, lstk, with_rho=True)
        Ss.barrier()
        rm1 = self.rm1
        with ExitStack() as lstk:
            mbc = self.sb(lstk, [128, TN], F32)
            rrbc = self.sb(lstk, [128, TN], F32)
            Ss.barrier(("tensor", "vector"))
            self.bcast_row(stat[0:1, 0:TN], mbc, self.pb[0], TN)
            self.bcast_row(stat[0:1, 2 * TN:3 * TN], rrbc, self.pb[0], TN)
            self.prescale(src, TN, mbc, rrbc, self.wln, rm1, TN, lstk, "rm1")
            Ss.barrier()
        wbufs = [self.sb(stk, [128, 2048], BF16) for _ in range(2)]
        ring = [self.sb(stk, [128, TN], BF16) for _ in range(2)]
        self.accq = self.sb(stk, [128, 1024], F32)
        m_list = list(range(41)) if full else [16, 17, 18, 19, 20, 21, 22, 23, 40]
        conv_tok = {}
        evac_state = {"blk": {}}

        def rhs_fn(k, n0, nsz):
            return rm1[:, k * TN + n0:k * TN + n0 + nsz]

        def evac(m, ni, n0, nsz, ps, tdone):
            po, do_, dsz = self.crop_store(n0, nsz)
            if m < 16:
                Ss.wait("scalar", "t", tdone)
                cnt = Ss.emit("scalar", lambda e, p=ps, mm=m:
                              e.activation(self.siluz[:, mm * T + do_:mm * T + do_ + dsz],
                                           p[:, po:po + dsz], AF.Silu), inc="a")
                return ("a", cnt)
            if m == 40:
                Ss.wait("vector", "t", tdone)
                cnt = Ss.emit("vector", lambda e, p=ps:
                              e.tensor_copy(self.rowsB[0:32, do_:do_ + dsz],
                                            p[0:32, po:po + dsz]), inc="v")
                return ("v", cnt)
            c = m - 16
            rb = ring[c % 2]
            if c >= 2 and ni == 0 and (c - 2) in conv_tok:
                Ss.wait("vector", *conv_tok[c - 2])
            Ss.wait("vector", "t", tdone)
            cnt = Ss.emit("vector", lambda e, p=ps, r=rb, nn0=n0, nns=nsz:
                          e.tensor_copy(r[:, nn0:nn0 + nns], p[:, 0:nns]),
                          inc="v")
            if ni == len(NTS_H) - 1:
                conv_tok[c] = self.conv_chunk(c, rb, stk)
            return ("v", cnt)

        self.stream_mm("winT", m_list, rhs_fn, evac, (self.pb[0], self.pb[1]),
                       wbufs, 16, nts=NTS_H)
        Ss.barrier()
        with ExitStack() as lstk:
            self.dt_rows(lstk)
        Ss.barrier()
        stk.close()

    def scan(self, stk, state_only):
        Ss = self.S
        Bz = self.sb(stk, [128, 16 * Q], BF16)
        Xt = [self.sb(stk, [128, 128], BF16) for _ in range(4)]
        Bt = [self.sb(stk, [128, 128], BF16) for _ in range(4)]
        wld = self.sb(stk, [128, 64], F32)
        if not state_only:
            lflat = self.sb(stk, [1, 32 * Q], F32)
            eflat = self.sb(stk, [1, 32 * Q], F32)
            e4 = self.sb(stk, [128, 512], BF16)
            gt4 = self.sb(stk, [128, 512], BF16)
            ebc = self.sb(stk, [128, 256], F32)
            ytmp = self.sb(stk, [128, 256], F32)
        wxtb = [self.sb(stk, [128, 512], BF16) for _ in range(2)]
        pbc = self.sb(stk, [64, 32], F32)
        pflat = self.sb(stk, [1, 32], F32)
        hnb = self.Hn2b
        Hn = self.Hn
        rA = self.rowsA
        em, wt, dma = Ss.emit, Ss.wait, Ss.dma
        Ss.emit("vector", lambda e: e.memset(Bz[:], 0.0), inc="v")
        Ss.barrier(("scalar", "sync", "gpsimd"))
        for i in range(NCH):
            i0 = i * Q
            base_scd = Ss.now("scd")
            if not state_only:
                for g in range(8):
                    sh = (g % 2) * 64
                    for par in range(2):
                        tile = g * 2 + par
                        dma("scalar",
                            Bz[par * 64:par * 64 + 64, tile * Q:(tile + 1) * Q],
                            self.xbc[sh:sh + 64,
                                     (4 + g // 2) * T + i0:(4 + g // 2) * T + i0 + Q],
                            "scd")
            if not state_only:
                dma("scalar", lflat[0:1].rearrange("p (h q) -> p h q", h=32),
                    rA[0:32, 0 * T + i0:0 * T + i0 + Q], "scd")
                dma("scalar", eflat[0:1].rearrange("p (h q) -> p h q", h=32),
                    rA[0:32, 1 * T + i0:1 * T + i0 + Q], "scd")
            dma("scalar", pflat[0:1].rearrange("p (h q) -> p h q", h=32),
                rA[0:32, 1 * T + i0 + Q - 1:1 * T + i0 + Q], "scd")
            scd_done = Ss.now("scd")
            vrel = Ss.now("v")
            wt("tensor", "v", vrel)
            slots = []
            for t4 in range(4):
                for src_c, dstt in ((t4, Xt[t4]), (4 + t4, Bt[t4])):
                    sl = len(slots)
                    em("tensor", lambda e, sc=src_c, ii=i0, s=sl: e.transpose(
                        self.pbT[:, s * 128:s * 128 + 128],
                        self.xbc[:, sc * T + ii:sc * T + ii + Q], self.identb[:]),
                        inc=("t" if sl == 7 else None))
                    slots.append(dstt)
            wt("vector", "t", Ss.now("t"))
            for sl, dstt in enumerate(slots):
                vrel = em("vector", lambda e, d=dstt, s=sl: e.tensor_copy(
                    d[:], self.pbT[:, s * 128:s * 128 + 128]),
                    inc=("v" if sl == 7 else None))
            em("tensor", lambda e, ii=i0: e.transpose(
                self.pb[6][:, 0:32], rA[0:32, 2 * T + ii:2 * T + ii + Q],
                self.identf[0:32, 0:32]), inc="t")
            wt("vector", "t", Ss.now("t"))
            vrel = em("vector", lambda e: e.tensor_copy(wld[:, 0:32],
                                                        self.pb[6][:, 0:32]),
                      inc="v")
            wt("tensor", "v", vrel)
            em("tensor", lambda e, ii=i0: e.transpose(
                self.pb[6][:, 32:64], rA[0:32, 3 * T + ii:3 * T + ii + Q],
                self.identf[0:32, 0:32]), inc="t")
            wt("vector", "t", Ss.now("t"))
            vrel = em("vector", lambda e: e.tensor_copy(wld[:, 32:64],
                                                        self.pb[6][:, 32:64]),
                      inc="v")
            wt("tensor", "scd", scd_done)
            wt("tensor", "v", vrel)
            em("tensor", lambda e: e.matmul(
                self.pb[6][0:64, 64:96], self.ones128[0:1, 0:64],
                pflat[0:1, 0:32], start=True, stop=True), inc="t")
            wt("vector", "t", Ss.now("t"))
            vrel = em("vector", lambda e: e.tensor_copy(
                pbc[:], self.pb[6][0:64, 64:96]), inc="v")
            wt("scalar", "v", vrel)
            hnb4 = hnb[:].rearrange("p (h2 two n) -> p h2 two n", two=2, n=64)
            hn4 = Hn[:].rearrange("p (h2 two n) -> p h2 two n", two=2, n=64)
            v0_ = em("vector", lambda e: e.memset(hnb[:], 0.0), inc="v")
            wt("scalar", "v", v0_)
            a0 = em("scalar", lambda e: e.activation(
                hnb4[0:64, :, 0:1, :], hn4[:, :, 0:1, :], AF.Identity), inc="a")
            wt("gpsimd", "a", a0)
            wt("gpsimd", "v", v0_)
            gcnt = Ss.dma("gpsimd", hnb4[64:128, :, 1:2, :], hn4[:, :, 1:2, :],
                          "gpd")
            wt("tensor", "gpd", gcnt)
            wt("tensor", "a", a0)
            for grp in range(8):
                g = grp
                h0 = grp * 4
                xsl = self.xbc[(g % 2) * 64:(g % 2) * 64 + 64,
                               (g // 2) * T + i0:(g // 2) * T + i0 + Q]
                xtsl = Xt[g // 2][:, (g % 2) * 64:(g % 2) * 64 + 64]
                btsl = Bt[g // 2][:, (g % 2) * 64:(g % 2) * 64 + 64]
                if not state_only and SCAN_PART >= 1:
                    for hh in range(4):
                        em("tensor", lambda e, h=h0 + hh, cc=hh: e.matmul(
                            self.pb[0][:, cc * 128:cc * 128 + 128], self.ones128,
                            lflat[0:1, h * Q:(h + 1) * Q], start=True, stop=True),
                            inc=("t" if hh == 3 else None))
                    wt("vector", "t", Ss.now("t"))
                    for hh in range(4):
                        em("vector", lambda e, h=h0 + hh, cc=hh: e.tensor_scalar(
                            e4[:, cc * 128:cc * 128 + 128],
                            self.pb[0][:, cc * 128:cc * 128 + 128],
                            wld[:, 32 + h:33 + h], CLAMP, OP.add, OP.min),
                            inc=("v" if hh == 3 else None))
                    wt("scalar", "v", Ss.now("v"))
                    acnt = em("scalar", lambda e: e.activation(e4[:], e4[:], AF.Exp),
                              inc="a")
                    wt("vector", "a", acnt)
                    em("vector", lambda e: e.tensor_tensor(e4[:], e4[:],
                                                           self.tri4[:], OP.mult),
                       inc="v")
                if not state_only and SCAN_PART >= 2:
                    for hh in range(4):
                        h = h0 + hh
                        tile = g * 2 + (h % 2)
                        bsl = Bz[:, tile * Q:(tile + 1) * Q]
                        csl = self.xbc[:, (8 + h // 2) * T + i0:
                                       (8 + h // 2) * T + i0 + Q]
                        em("tensor", lambda e, b=bsl, cs=csl, cc=hh:
                           e.matmul(self.pb[1][:, cc * 128:cc * 128 + 128], b, cs,
                                    start=True, stop=True),
                           inc=("t" if hh == 3 else None))
                    wt("vector", "t", Ss.now("t"))
                    wt("vector", "a", Ss.now("a"))
                    if SCAN_PART >= 20:
                        vcnt = em("vector", lambda e: e.tensor_tensor(
                            gt4[:], self.pb[1][:, 0:512], e4[:], OP.mult),
                            inc="v")
                        wt("tensor", "v", vcnt)
                if not state_only and SCAN_PART >= 3:
                    for hh in range(4):
                        h = h0 + hh
                        cb = (h % 2)
                        slot = (hh // 2) * 128
                        em("tensor", lambda e, xt=xtsl, cc=hh, sl=slot, base=cb * 64:
                           e.matmul(self.pb[2][base:base + 64, sl:sl + 128], xt,
                                    gt4[:, cc * 128:cc * 128 + 128],
                                    start=True, stop=False))
                        em("tensor", lambda e, h_=h, sl=slot, base=cb * 64,
                           xs=self.xbc[:, (g // 2) * T + i0:(g // 2) * T + i0 + Q]:
                           e.matmul(self.pb[2][base:base + 64, sl:sl + 128],
                                    self.ddiag[:, h_ * 64:h_ * 64 + 64],
                                    xs, start=False, stop=True))
                        csl = self.xbc[:, (8 + h // 2) * T + i0:
                                       (8 + h // 2) * T + i0 + Q]
                        em("tensor", lambda e, h_=h, sl=slot, base=cb * 64, cs=csl:
                           e.matmul(self.pb[3][base:base + 64, sl:sl + 128],
                                    hnb[:, h_ * 64:h_ * 64 + 64], cs,
                                    start=True, stop=True))
                        em("tensor", lambda e, h_=h, sl=slot, top=(hh % 2 == 0):
                           e.matmul(self.pb[0][:, sl:sl + 128],
                                    self.top64 if top else self.bot64,
                                    eflat[0:1, h_ * Q:(h_ + 1) * Q],
                                    start=top, stop=not top),
                           inc=("t" if hh == 3 else None))
                    tnow = Ss.now("t")
                if not state_only and SCAN_PART >= 4:
                    tnow = Ss.now("t")
                    wt("vector", "t", tnow)
                    em("vector", lambda e: e.tensor_copy(ebc[:], self.pb[0][:, 0:256]))
                    em("vector", lambda e: e.tensor_tensor(
                        ytmp[:], self.pb[3][:, 0:256], ebc[:], OP.mult))
                    for sl2 in range(2):
                        ch = 2 * g + sl2
                        vcnt = em("vector", lambda e, c=ch, s=sl2, ii=i0:
                                  e.tensor_tensor(
                                      self.ya[:, c * T + ii:c * T + ii + Q],
                                      ytmp[:, s * 128:s * 128 + 128],
                                      self.pb[2][:, s * 128:s * 128 + 128],
                                      OP.add),
                            inc="v")
                    wt("tensor", "v", vcnt)
            wxfree = [None, None]
            hnfree = [None, None]
            for r in range(4):
                if wxfree[r % 2] is not None:
                    wt("scalar", "t", wxfree[r % 2])
                atok = None
                for hh in range(8):
                    h = 8 * r + hh
                    g = h // 4
                    xts = Xt[g // 2][:, (g % 2) * 64:(g % 2) * 64 + 64]
                    atok = em("scalar", lambda e, h_=h, xt=xts, cc=hh,
                              b=wxtb[r % 2]: e.activation(
                                  b[:, cc * 64:cc * 64 + 64], xt, AF.Identity,
                                  scale=wld[:, h_:h_ + 1]), inc="a")
                wt("tensor", "a", atok)
                if hnfree[r % 2] is not None:
                    wt("tensor", "v", hnfree[r % 2])
                pbR = self.pb[4 + r % 2]
                for hh in range(8):
                    h = 8 * r + hh
                    g = h // 4
                    bts = Bt[g // 2][:, (g % 2) * 64:(g % 2) * 64 + 64]
                    em("tensor", lambda e, bt=bts, cc=hh, b=wxtb[r % 2],
                       p=pbR: e.matmul(
                           p[0:64, cc * 64:cc * 64 + 64], bt,
                           b[:, cc * 64:cc * 64 + 64],
                           start=True, stop=True),
                       inc=("t" if hh == 7 else None))
                wxfree[r % 2] = Ss.now("t")
                wt("vector", "t", wxfree[r % 2])
                vcnt = None
                for hh in range(8):
                    h = 8 * r + hh
                    vcnt = em("vector", lambda e, h_=h, cc=hh, p=pbR:
                              e.scalar_tensor_tensor(
                                  Hn[:, h_ * 64:h_ * 64 + 64],
                                  Hn[:, h_ * 64:h_ * 64 + 64],
                                  pbc[:, h_:h_ + 1],
                                  p[0:64, cc * 64:cc * 64 + 64],
                                  OP.mult, OP.add),
                              inc=("v" if hh == 7 else None))
                hnfree[r % 2] = vcnt
            Ss.barrier(("tensor", "vector", "scalar"))

    def rms_stats_res(self, arena, stride, nch, rout, stk):
        Ss = self.S
        sq = self.sb(stk, [128, T], F32)
        scr = self.sb(stk, [1, T], F32)
        epsc = self.sb(stk, [1, 1], F32)
        Ss.emit("vector", lambda e: e.memset(epsc[:], EPS), inc="v")
        tprev = 0
        for c in range(nch):
            if c >= 1:
                Ss.wait("scalar", "t", tprev)
            acnt = Ss.emit("scalar", lambda e, cc=c: e.activation(
                sq[:], arena[:, cc * stride:cc * stride + T], AF.Square), inc="a")
            Ss.wait("tensor", "a", acnt)
            for si, (s0, sz) in enumerate(NTS):
                Ss.emit("tensor", lambda e, p=self.pb[5 + si], s0_=s0, sz_=sz,
                        st=(c == 0), sp=(c == nch - 1):
                        e.matmul(p[0:1, 0:sz_], self.onescol[:, 0:1],
                                 sq[:, s0_:s0_ + sz_], start=st, stop=sp),
                        inc="t")
            tprev = Ss.now("t")
        Ss.wait("vector", "t", tprev)
        for si, (s0, sz) in enumerate(NTS):
            Ss.emit("vector", lambda e, p=self.pb[5 + si], s0_=s0, sz_=sz:
                    e.tensor_copy(scr[0:1, s0_:s0_ + sz_], p[0:1, 0:sz_]),
                    inc="v")
        em, wt = Ss.emit, Ss.wait
        v = em("vector", lambda e: e.tensor_single_scalar(
            scr[0:1, 0:T], scr[0:1, 0:T], 1.0 / 2048.0, OP.mult), inc="v")
        wt("scalar", "v", v)
        a = em("scalar", lambda e: e.activation(scr[0:1, 0:T], scr[0:1, 0:T],
                                                AF.Sqrt, bias=epsc[0:1, 0:1]),
               inc="a")
        wt("vector", "a", a)
        v = em("vector", lambda e: e.reciprocal(rout, scr[0:1, 0:T]), inc="v")
        return ("v", v)

    def ln_stats_res(self, arena, stride, mout, rout, stk):
        Ss = self.S
        sq = self.sb(stk, [128, T], F32)
        scr = self.sb(stk, [1, 2 * T], F32)
        epsc = self.sb(stk, [1, 1], F32)
        Ss.emit("vector", lambda e: e.memset(epsc[:], EPS), inc="v")
        ps_s = [self.pb[2], self.pb[3]]
        ps_q = [self.pb[5], self.pb[6]]
        tprev = 0
        for c in range(16):
            if c >= 1:
                Ss.wait("scalar", "t", tprev)
            acnt = Ss.emit("scalar", lambda e, cc=c: e.activation(
                sq[:], arena[:, cc * stride:cc * stride + T], AF.Square), inc="a")
            Ss.wait("tensor", "a", acnt)
            for si, (s0, sz) in enumerate(NTS):
                for psm, gs in ((ps_s[si], 0), (ps_q[si], 1)):
                    Ss.emit("tensor", lambda e, p=psm, g=gs, cc=c, s0_=s0,
                            sz_=sz, st=(c == 0), sp=(c == 15):
                            e.matmul(p[0:1, 0:sz_], self.onescol[:, 0:1],
                                     (arena[:, cc * stride + s0_:
                                            cc * stride + s0_ + sz_]
                                      if g == 0 else sq[:, s0_:s0_ + sz_]),
                                     start=st, stop=sp),
                            inc="t")
            tprev = Ss.now("t")
        Ss.wait("vector", "t", tprev)
        for si, (s0, sz) in enumerate(NTS):
            Ss.emit("vector", lambda e, p=ps_s[si], s0_=s0, sz_=sz:
                    e.tensor_copy(scr[0:1, s0_:s0_ + sz_], p[0:1, 0:sz_]))
            Ss.emit("vector", lambda e, p=ps_q[si], s0_=s0, sz_=sz:
                    e.tensor_copy(scr[0:1, T + s0_:T + s0_ + sz_],
                                  p[0:1, 0:sz_]), inc="v")
        s1 = scr[0:1, 0:T]
        s2 = scr[0:1, T:2 * T]
        em, wt = Ss.emit, Ss.wait
        inv_d = 1.0 / 2048.0
        em("vector", lambda e: e.tensor_single_scalar(mout, s1, inv_d, OP.mult))
        em("vector", lambda e: e.tensor_single_scalar(s2, s2, inv_d, OP.mult))
        em("vector", lambda e: e.tensor_tensor(s1, mout, mout, OP.mult))
        v = em("vector", lambda e: e.tensor_tensor(s2, s2, s1, OP.subtract),
               inc="v")
        wt("scalar", "v", v)
        a = em("scalar", lambda e: e.activation(s1, s2, AF.Sqrt,
                                                bias=epsc[0:1, 0:1]), inc="a")
        wt("vector", "a", a)
        v = em("vector", lambda e: e.reciprocal(rout, s1), inc="v")
        return ("v", v)

    def p3(self, stk):
        Ss = self.S
        for c in range(16):
            Ss.emit("vector", lambda e, cc=c: e.tensor_tensor(
                self.ya[:, cc * T:(cc + 1) * T], self.ya[:, cc * T:(cc + 1) * T],
                self.siluz[:, cc * T:(cc + 1) * T], OP.mult),
                inc=("v" if c == 15 else None))
        Ss.barrier()
        rho3 = self.sb(stk, [1, T], F32)
        with ExitStack() as sub:
            self.rms_stats_res(self.ya, T, 16, rho3[0:1, 0:T], sub)
        Ss.barrier()
        rbc3 = self.sb(stk, [128, T], F32)
        self.bcast_row(rho3[0:1, 0:T], rbc3, self.pb[0], T)
        Ss.barrier()
        tmpf = self.sb(stk, [128, T], F32)
        aprev = 0
        for c in range(16):
            if c >= 1:
                Ss.wait("vector", "a", aprev)
            v = Ss.emit("vector", lambda e, cc=c: e.tensor_tensor(
                tmpf[:], self.ya[:, cc * T:(cc + 1) * T], rbc3[:], OP.mult),
                inc="v")
            Ss.wait("scalar", "v", v)
            aprev = Ss.emit("scalar", lambda e, cc=c: e.activation(
                self.ya[:, cc * T:(cc + 1) * T], tmpf[:], AF.Identity,
                scale=self.ssmw[:, cc:cc + 1]), inc="a")
        Ss.barrier()
        mbc = self.sb(stk, [128, T], F32)
        rbc = self.sb(stk, [128, T], F32)
        self.bcast_row(self.stat_main[0:1, 3:3 + T], mbc, self.pb[0], T)
        self.bcast_row(self.stat_main[0:1, TN + 3:TN + 3 + T], rbc, self.pb[0], T)
        Ss.barrier()
        wbufs = [self.sb(stk, [128, 2048], BF16) for _ in range(2)]
        xb = [self.sb(stk, [128, T], F32) for _ in range(2)]
        ut = self.sb(stk, [128, 512], F32)
        xsrc = self.hbm["xT"]
        st2x = Stream2(self, "sync", "p3x")
        st2x.load(0, xb[0][:], xsrc[0:128, 3:TN])
        st2x.load(1, xb[1][:], xsrc[128:256, 3:TN])

        def rhs_fn(k, n0, nsz):
            return self.ya[:, k * T + n0:k * T + n0 + nsz]

        def evac(m, ni, n0, nsz, ps, tdone):
            st2x.wait("vector", m % 2)
            Ss.wait("vector", "t", tdone)
            Ss.emit("vector", lambda e, b=xb[m % 2], nn0=n0, nns=nsz:
                    e.tensor_tensor(ut[:, 0:nns], b[:, nn0:nn0 + nns],
                                    mbc[:, nn0:nn0 + nns], OP.subtract))
            Ss.emit("vector", lambda e, nn0=n0, nns=nsz:
                    e.tensor_tensor(ut[:, 0:nns], ut[:, 0:nns],
                                    rbc[:, nn0:nn0 + nns], OP.mult))
            cnt = Ss.emit("vector", lambda e, p=ps, mm=m, nn0=n0, nns=nsz:
                          e.tensor_tensor(
                              self.r2a[:, mm * T + nn0:mm * T + nn0 + nns],
                              ut[:, 0:nns], p[:, 0:nns], OP.add), inc="v")
            if ni == 1 and m + 2 < 16:
                st2x.load(m % 2, xb[m % 2][:],
                          xsrc[(m + 2) * 128:(m + 3) * 128, 3:TN],
                          guard=("v", cnt))
            return ("v", cnt)

        self.stream_mm("woutT", list(range(16)), rhs_fn, evac,
                       (self.pb[0], self.pb[1]), wbufs, 16)
        Ss.barrier()

    def p4(self, stk):
        Ss = self.S
        with ExitStack() as sub:
            rho4 = self.sb(sub, [1, T], F32)
            with ExitStack() as sub2:
                self.rms_stats_res(self.r2a, T, 16, rho4[0:1, 0:T], sub2)
            Ss.barrier()
            rbc4 = self.sb(sub, [128, T], F32)
            self.bcast_row(rho4[0:1, 0:T], rbc4, self.pb[0], T)
            Ss.barrier()
            tmpf = self.sb(sub, [128, T], F32)
            aprev = 0
            for c in range(16):
                if c >= 1:
                    Ss.wait("vector", "a", aprev)
                v = Ss.emit("vector", lambda e, cc=c: e.tensor_tensor(
                    tmpf[:], self.r2a[:, cc * T:(cc + 1) * T], rbc4[:], OP.mult),
                    inc="v")
                Ss.wait("scalar", "v", v)
                aprev = Ss.emit("scalar", lambda e, cc=c: e.activation(
                    self.rms2[:, cc * T:(cc + 1) * T], tmpf[:], AF.Identity,
                    scale=self.wpost[:, cc:cc + 1]), inc="a")
            Ss.barrier()
            xb = [self.sb(sub, [128, T], F32) for _ in range(2)]
            xsrc = self.hbm["xT"]
            st2x = Stream2(self, "sync", "p4x")
            st2x.load(0, xb[0][:], xsrc[0:128, 3:TN])
            st2x.load(1, xb[1][:], xsrc[128:256, 3:TN])
            for c in range(16):
                st2x.wait("vector", c % 2)
                v = Ss.emit("vector", lambda e, cc=c, b=xb[c % 2]:
                            e.tensor_tensor(
                                self.r2a[:, cc * T:(cc + 1) * T],
                                self.r2a[:, cc * T:(cc + 1) * T],
                                b[:], OP.add), inc="v")
                if c + 2 < 16:
                    st2x.load(c % 2, xb[c % 2][:],
                              xsrc[(c + 2) * 128:(c + 3) * 128, 3:TN],
                              guard=("v", v))
            Ss.barrier()
        g8a = self.sb(stk, [128, 16 * T], BF16)
        g8b = self.sb(stk, [128, 16 * T], BF16)

        def g8c(c):
            return (g8a if c < 16 else g8b, (c % 16) * T)
        wg = [self.sb(stk, [128, 2048], BF16) for _ in range(2)]
        wu = [self.sb(stk, [128, 2048], BF16) for _ in range(2)]
        wd = [self.sb(stk, [128, 4096], BF16) for _ in range(2)]
        stile = self.sb(stk, [128, 512], BF16)
        for half in range(2):
            self._swiglu_gu(half, g8c, wg, wu, stk, stile)
            Ss.barrier()
            self._down_half(half, g8c, wd, "wdT", None)
            Ss.barrier()

    def _swiglu_gu(self, half, g8, wg, wu, stk, stile):
        Ss = self.S
        gap = self.hbm["wgT"]
        uap = self.hbm["wuT"]
        m_list = list(range(half * 32, (half + 1) * 32))
        st2 = Stream2(self, "gpsimd", f"wg{self.uid()}_")
        buf_use = [0, 0]

        def load(mi):
            g = ("t", buf_use[mi % 2]) if buf_use[mi % 2] else None
            st2.load(mi % 2, wg[mi % 2][:], gap[m_list[mi]], guard=g)
            st2.load(mi % 2, wu[mi % 2][:], uap[m_list[mi]])
        load(0)
        if len(m_list) > 1:
            load(1)
        ps_rel = None
        for mi, m in enumerate(m_list):
            st2.wait("tensor", mi % 2)
            for ni, (n0, nsz) in enumerate(NTS):
                if ps_rel is not None:
                    Ss.wait("tensor", *ps_rel)
                for pj, wb in ((self.pb[0], wg[mi % 2]), (self.pb[1], wu[mi % 2])):
                    for j in range(16):
                        Ss.emit("tensor", lambda e, p=pj, w=wb, jj=j, nn0=n0,
                                nns=nsz, st=(j == 0), sp=(j == 15):
                                e.matmul(p[:, 0:nns], w[:, jj * 128:(jj + 1) * 128],
                                         self.rms2[:, jj * T + nn0:jj * T + nn0 + nns],
                                         start=st, stop=sp),
                                inc=("t" if j == 15 else None))
                tdone = Ss.now("t")
                buf_use[mi % 2] = tdone
                Ss.wait("scalar", "t", tdone)
                a = Ss.emit("scalar", lambda e, nns=nsz: e.activation(
                    stile[:, 0:nns], self.pb[0][:, 0:nns], AF.Silu), inc="a")
                Ss.wait("vector", "a", a)
                gt_, go_ = g8(m - half * 32)
                cnt = Ss.emit("vector", lambda e, gg=gt_, oo=go_, nn0=n0, nns=nsz:
                              e.tensor_tensor(
                                  gg[:, oo + nn0:oo + nn0 + nns],
                                  stile[:, 0:nns], self.pb[1][:, 0:nns], OP.mult),
                              inc="v")
                ps_rel = ("v", cnt)
            if mi + 2 < len(m_list):
                load(mi + 2)

    def _down_half(self, half, g8, wd, wname, scale_bc):
        Ss = self.S
        wap = self.hbm[wname]
        st2 = Stream2(self, "gpsimd", f"wd{self.uid()}_")
        buf_use = [0, 0]

        def load2(mi):
            g = ("t", buf_use[mi % 2]) if buf_use[mi % 2] else None
            st2.load(mi % 2, wd[mi % 2][:],
                     wap[mi, :, half * 4096:(half + 1) * 4096], guard=g)
        load2(0)
        load2(1)
        ps_rel = [None, None]
        it = 0
        for m in range(16):
            st2.wait("tensor", m % 2)
            for ni, (n0, nsz) in enumerate(NTS):
                pi = it % 2
                ps = self.pb[pi]
                if ps_rel[pi] is not None:
                    Ss.wait("tensor", *ps_rel[pi])
                for j in range(32):
                    gt_, go_ = g8(j)
                    Ss.emit("tensor", lambda e, p=ps, w=wd[m % 2], jj=j, nn0=n0,
                            nns=nsz, gg=gt_, oo=go_, st=(j == 0), sp=(j == 31):
                            e.matmul(p[:, 0:nns], w[:, jj * 128:(jj + 1) * 128],
                                     gg[:, oo + nn0:oo + nn0 + nns],
                                     start=st, stop=sp),
                            inc=("t" if j == 31 else None))
                tdone = Ss.now("t")
                buf_use[m % 2] = tdone
                Ss.wait("vector", "t", tdone)
                cnt = Ss.emit("vector", lambda e, p=ps, mm=m, nn0=n0, nns=nsz:
                              e.tensor_tensor(
                                  self.r2a[:, mm * T + nn0:mm * T + nn0 + nns],
                                  self.r2a[:, mm * T + nn0:mm * T + nn0 + nns],
                                  p[:, 0:nns], OP.add), inc="v")
                ps_rel[pi] = ("v", cnt)
                it += 1
            if m + 2 < 16:
                load2(m + 2)

    def ladder(self, src_psum_or_sbuf, scbc, n0, nsz, dst_ap, v0, vt, ct,
               from_psum=True):
        Ss = self.S
        em, wt = Ss.emit, Ss.wait
        em("vector", lambda e: e.tensor_tensor(
            v0[:, 0:nsz], src_psum_or_sbuf, scbc, OP.mult))
        em("vector", lambda e: e.tensor_copy(vt[:, 0:nsz], v0[:, 0:nsz]))
        for j, k in enumerate((128, 64, 32, 16, 8, 4, 2, 1)):
            em("vector", lambda e, jj=j, kk=k: e.tensor_scalar(
                ct[:, 0:nsz], vt[:, 0:nsz], self.ladthr[:, jj:jj + 1],
                -float(kk), OP.is_ge, OP.mult))
            em("vector", lambda e: e.tensor_tensor(
                vt[:, 0:nsz], vt[:, 0:nsz], ct[:, 0:nsz], OP.add))
        cnt = em("vector", lambda e: e.scalar_tensor_tensor(
            dst_ap, v0[:, 0:nsz], 128.0, vt[:, 0:nsz], OP.subtract, OP.subtract),
            inc="v")
        return ("v", cnt)

    def ladder2_pair(self, srcs, sccols, dsts, bufs, ptoks):
        Ss = self.S
        em, wt = Ss.emit, Ss.wait
        N = len(srcs)
        cur = [None] * N
        first = [None] * N
        for i in range(N):
            wt("vector", *ptoks[i])
            first[i] = em("vector", lambda e, ii=i: e.tensor_tensor(
                bufs[ii][0][:], srcs[ii], sccols[ii], OP.mult), inc="v")
            cur[i] = ("v", first[i])
        prev_a = [None] * N
        for j, k in enumerate(LAD_KS):
            if j in LAD_ACT:
                atok = [None] * N
                for i in range(N):
                    v0, wtb, ct = bufs[i]
                    s = v0 if j == 0 else wtb
                    wt("scalar", *cur[i])
                    atok[i] = em("scalar", lambda e, ss=s, cc=ct, jj=j:
                                 e.activation(cc[:], ss[:], AF.Sign,
                                              bias=self.ladthr[:, 16 + jj:17 + jj]),
                                 inc="a")
                    prev_a[i] = atok[i]
                for i in range(N):
                    v0, wtb, ct = bufs[i]
                    s = v0 if j == 0 else wtb
                    wt("vector", "a", atok[i])
                    if cur[i][0] == "g":
                        wt("vector", *cur[i])
                    cur[i] = ("v", em(
                        "vector", lambda e, ss=s, ww=wtb, cc=ct, kk=k:
                        e.scalar_tensor_tensor(ww[:], cc[:], -0.5 * kk,
                                               ss[:], OP.mult, OP.add),
                        inc="v"))
            else:
                vtok = [None] * N
                for i in range(N):
                    v0, wtb, ct = bufs[i]
                    s = v0 if j == 0 else wtb
                    if cur[i][0] == "g":
                        wt("vector", *cur[i])
                    vtok[i] = em("vector", lambda e, ss=s, cc=ct, jj=j, kk=k:
                                 e.tensor_scalar(cc[:], ss[:],
                                                 self.ladthr[:, 8 + jj:9 + jj],
                                                 -float(kk), OP.is_ge, OP.mult),
                                 inc="v")
                for i in range(N):
                    v0, wtb, ct = bufs[i]
                    s = v0 if j == 0 else wtb
                    wt("gpsimd", "v", vtok[i])
                    if prev_a[i] is not None:
                        wt("gpsimd", "a", prev_a[i])
                    cur[i] = ("g", em(
                        "gpsimd", lambda e, ss=s, ww=wtb, cc=ct:
                        e.tensor_tensor(ww[:], ss[:], cc[:], OP.add),
                        inc="g"))
        fin = None
        for i in range(N):
            v0, wtb, ct = bufs[i]
            if cur[i][0] == "g":
                wt("vector", *cur[i])
            fin = em("vector", lambda e, vv=v0, ww=wtb, dd=dsts[i]:
                     e.scalar_tensor_tensor(dd, vv[:], LAD_FC, ww[:],
                                            OP.subtract, OP.subtract), inc="v")
        return first, fin

    def token_max_row(self, runmx, runmn, mrow, scrow2, stk):
        Ss = self.S
        mcol = self.sb(stk, [128, 16], F32)
        m8 = self.sb(stk, [16, 128], F32)
        vrel = Ss.now("v")
        for tb in range(8):
            for pi, (src_t, red) in enumerate(((runmx, 0), (runmn, 1))):
                Ss.wait("tensor", "v", vrel)
                Ss.emit("tensor", lambda e, b=tb, s=src_t: e.transpose(
                    self.pb[0][:, 0:128], s[:, b * 128:(b + 1) * 128],
                    self.identf[:]), inc="t")
                Ss.wait("vector", "t", Ss.now("t"))
                if red == 0:
                    vrel = Ss.emit("vector", lambda e, b=tb: e.reduce_max(
                        mcol[:, b:b + 1], self.pb[0][:, 0:128],
                        axis=mybir.AxisListType.X), inc="v")
                else:
                    vrel = Ss.emit("vector", lambda e, b=tb: e.tensor_reduce(
                        mcol[:, 8 + b:9 + b], self.pb[0][:, 0:128],
                        mybir.AxisListType.X, OP.min), inc="v")
        Ss.wait("tensor", "v", vrel)
        Ss.emit("tensor", lambda e: e.transpose(
            self.pb[0][0:16, 0:128], mcol[:, 0:16], self.identf[:]), inc="t")
        Ss.wait("vector", "t", Ss.now("t"))
        v = Ss.emit("vector", lambda e: e.tensor_copy(m8[:],
                                                      self.pb[0][0:16, 0:128]),
                    inc="v")
        Ss.wait("scalar", "v", v)
        Ss.dma("scalar", mrow[0:1].rearrange("p (tb q) -> p tb q", tb=8),
               m8[0:8, :], "scd")
        Ss.dma("scalar", scrow2[0:1].rearrange("p (tb q) -> p tb q", tb=8),
               m8[8:16, :], "scd")
        Ss.wait("vector", "scd", Ss.now("scd"))
        Ss.emit("vector", lambda e: e.tensor_single_scalar(
            scrow2, scrow2, -1.0, OP.mult))
        v = Ss.emit("vector", lambda e: e.tensor_tensor(
            mrow[0:1, 0:T], mrow[0:1, 0:T], scrow2, OP.max), inc="v")

    def p5(self, stk, sC, sXQ):
        Ss = self.S
        ln2 = self.sb(stk, [128, 16 * T], BF16, "ln2")
        stat = self.sb(stk, [1, 2 * T], F32)
        with ExitStack() as l:
            self.ln_stats_res(self.r2a, T, stat[0:1, 0:T], stat[0:1, T:2 * T], l)
        Ss.barrier()
        mbc = self.sb(stk, [128, T], F32)
        rbc = self.sb(stk, [128, T], F32)
        self.bcast_row(stat[0:1, 0:T], mbc, self.pb[0], T)
        self.bcast_row(stat[0:1, T:2 * T], rbc, self.pb[0], T)
        Ss.barrier()
        tmpf = self.sb(stk, [128, T], F32)
        for c in range(16):
            Ss.emit("vector", lambda e, cc=c: e.tensor_tensor(
                tmpf[:], self.r2a[:, cc * T:(cc + 1) * T], mbc[:], OP.subtract))
            v = Ss.emit("vector", lambda e, cc=c: e.tensor_tensor(
                ln2[:, cc * T:(cc + 1) * T], tmpf[:], rbc[:], OP.mult), inc="v")
            Ss.wait("sync", "v", v)
            Ss.dma("sync", self.hbm["sch2"][c * 128:(c + 1) * 128, :],
                   self.r2a[:, c * T:(c + 1) * T], "st")
        Ss.barrier()
        sC.close()
        Y1 = self.sb(stk, [128, 16 * T], F32, "Y1")
        self.xq = self.sb(sXQ, [128, 16 * T], BF16, "xq", hi=True)
        xq = self.xq
        H16 = _hadamard(16)
        runmx = self.sb(stk, [128, T], F32)
        runmn = self.sb(stk, [128, T], F32)
        Ss.emit("vector", lambda e: e.memset(runmx[:], 0.0))
        Ss.emit("vector", lambda e: e.memset(runmn[:], 0.0), inc="v")
        Ss.barrier(("tensor", "vector"))
        ps_rel = [None, None]
        it = 0
        for ip in range(16):
            order = sorted(range(16), key=lambda k: -H16[ip, k])
            for ni, (n0, nsz) in enumerate(NTS):
                pi = it % 2
                ps = self.pb[pi]
                if ps_rel[pi] is not None:
                    Ss.wait("tensor", *ps_rel[pi])
                for j, k in enumerate(order):
                    plus = H16[ip, k] > 0
                    Ss.emit("tensor", lambda e, p=ps, kk=k, pl=plus, nn0=n0,
                            nns=nsz, st=(j == 0), sp=(j == 15):
                            e.matmul(p[:, 0:nns],
                                     self.h128pm[:, 0:128] if pl
                                     else self.h128pm[:, 128:256],
                                     ln2[:, kk * T + nn0:kk * T + nn0 + nns],
                                     start=st, stop=sp),
                            inc=("t" if j == 15 else None))
                tdone = Ss.now("t")
                Ss.wait("vector", "t", tdone)
                Ss.emit("vector", lambda e, p=ps, nn0=n0, nns=nsz:
                        e.tensor_tensor(runmx[:, nn0:nn0 + nns],
                                        runmx[:, nn0:nn0 + nns],
                                        p[:, 0:nns], OP.max))
                Ss.emit("vector", lambda e, p=ps, nn0=n0, nns=nsz:
                        e.tensor_tensor(runmn[:, nn0:nn0 + nns],
                                        runmn[:, nn0:nn0 + nns],
                                        p[:, 0:nns], OP.min))
                cnt = Ss.emit("vector", lambda e, p=ps, mm=ip, nn0=n0, nns=nsz:
                              e.tensor_copy(Y1[:, mm * T + nn0:mm * T + nn0 + nns],
                                            p[:, 0:nns]), inc="v")
                ps_rel[pi] = ("v", cnt)
                it += 1
        Ss.barrier()
        self.sSCROW = ExitStack()
        self.scrow = self.sb(self.sSCROW, [1, 2 * T], F32, hi=True)
        mrow = self.sb(stk, [1, 3 * T], F32)
        self.token_max_row(runmx, runmn, mrow[0:1, 0:T],
                           mrow[0:1, 2 * T:3 * T], stk)
        Ss.barrier(("vector",))
        rn = float(np.float32(np.sqrt(2048.0)))
        deq1 = self.scrow[0:1, 0:T]
        sc1 = mrow[0:1, T:2 * T]
        tmpr = mrow[0:1, 2 * T:3 * T]
        em = Ss.emit
        em("vector", lambda e: e.tensor_scalar(deq1, mrow[0:1, 0:T],
                                               self.c_invrn1, EPS,
                                               OP.mult, OP.max))
        em("vector", lambda e: e.tensor_single_scalar(tmpr, deq1, rn, OP.mult))
        em("vector", lambda e: e.reciprocal(sc1, tmpr))
        em("vector", lambda e: e.tensor_single_scalar(sc1, sc1, 127.0, OP.mult))
        v = em("vector", lambda e: e.tensor_single_scalar(deq1, deq1, 1.0 / 127.0,
                                                          OP.mult), inc="v")
        Ss.barrier()
        scbc = self.sb(stk, [128, T], F32)
        self.bcast_row(sc1, scbc, self.pb[0], T)
        Ss.barrier()
        v0 = self.sb(stk, [128, T], F32)
        vt = self.sb(stk, [128, T], F32)
        ct = self.sb(stk, [128, T], F32)
        for c in range(16):
            self.ladder(Y1[:, c * T:(c + 1) * T], scbc[:, 0:T], 0, T,
                        xq[:, c * T:(c + 1) * T], v0, vt, ct, from_psum=False)
        Ss.barrier()

    def p0(self, stk):
        Ss = self.S
        col = self.sb(stk, [128, 200], F32)
        self.wsc = self.sb(self.stk, [1, 8], F32, hi=True)
        self.abc3 = self.sb(self.stk, [128, 6], F32, hi=True)
        wt_ = [self.sb(stk, [128, 2048], F32) for _ in range(4)]
        dummy = self.sb(stk, [128, 2048], F32)
        srcs = []
        for m in range(64):
            srcs.append(self.hbm["fgT"][m])
        for m in range(64):
            srcs.append(self.hbm["fuT"][m])
        for m in range(16):
            for q4 in range(4):
                srcs.append(self.hbm["fdT"][m, :, q4 * 2048:(q4 + 1) * 2048])
        st2 = Stream2(self, "sync", "p0x", nslots=4)
        for s4 in range(4):
            st2.load(s4, wt_[s4][:], srcs[s4])
        for j, src in enumerate(srcs):
            if j % 2 == 0:
                st2.wait("scalar", j % 4)
                tok = ("a", Ss.emit(
                    "scalar", lambda e, b=wt_[j % 4], jj=j:
                    e.activation(dummy[:], b[:], AF.Abs,
                                 accum_out=col[:, jj:jj + 1]), inc="a"))
            else:
                st2.wait("vector", j % 4)
                tok = ("v", Ss.emit(
                    "vector", lambda e, b=wt_[j % 4], jj=j:
                    e.reduce_sum(col[:, jj:jj + 1], b[:],
                                 axis=mybir.AxisListType.X,
                                 apply_absolute_value=True), inc="v"))
            if j + 4 < len(srcs):
                st2.load(j % 4, wt_[j % 4][:], srcs[j + 4], guard=tok)
        Ss.barrier()
        tot = self.sb(stk, [128, 3], F32)
        for wi, (c0, cn) in enumerate(((0, 64), (64, 64), (128, 64))):
            Ss.emit("vector", lambda e, w=wi, a=c0, b=cn: e.reduce_sum(
                tot[:, w:w + 1], col[:, a:a + b], axis=mybir.AxisListType.X),
                inc="v")
        Ss.barrier(("tensor",))
        for wi in range(3):
            Ss.emit("tensor", lambda e, w=wi: e.matmul(
                self.pb[2][0:1, w:w + 1], self.onescol[:, 0:1], tot[:, w:w + 1],
                start=True, stop=True), inc="t")
        Ss.barrier(("vector",))
        v = Ss.emit("vector", lambda e: e.tensor_copy(self.wsc[0:1, 0:3],
                                                      self.pb[2][0:1, 0:3]),
                    inc="v")
        Ss.emit("vector", lambda e: e.tensor_scalar(
            self.wsc[0:1, 0:3], self.wsc[0:1, 0:3], self.c_nel, EPS,
            OP.mult, OP.max))
        v = Ss.emit("vector", lambda e: e.tensor_single_scalar(
            self.wsc[0:1, 3:6], self.wsc[0:1, 0:3], 0.5, OP.mult), inc="v")
        Ss.barrier(("tensor",))
        Ss.emit("tensor", lambda e: e.matmul(
            self.pb[2][:, 0:3], self.ones128, self.wsc[0:1, 3:6],
            start=True, stop=True), inc="t")
        Ss.barrier(("vector",))
        Ss.emit("vector", lambda e: e.tensor_copy(self.abc3[:, 0:3],
                                                  self.pb[2][:, 0:3]), inc="v")
        Ss.emit("vector", lambda e: e.tensor_single_scalar(
            self.abc3[:, 3:6], self.abc3[:, 0:3], -1.0, OP.mult), inc="v")
        Ss.barrier()

    def p6(self, stk, sF, sXQ):
        Ss = self.S
        em, wt, dma = Ss.emit, Ss.wait, Ss.dma
        xq = self.xq
        stk_gu = ExitStack()
        wtg = [self.sb(stk_gu, [128, 1024], F32) for _ in range(2)]
        wtu = [self.sb(stk_gu, [128, 1024], F32) for _ in range(2)]
        tqg = [self.sb(stk_gu, [128, 2048], BF16) for _ in range(2)]
        tqu = [self.sb(stk_gu, [128, 2048], BF16) for _ in range(2)]
        sgn = self.sb(stk_gu, [128, 1024], BF16)
        stile = self.sb(stk_gu, [128, 512], BF16)
        utmp = self.sb(stk_gu, [128, 512], F32)
        stg = [self.sb(stk_gu, [128, 1024], BF16) for _ in range(2)]
        gsbc = self.sb(stk_gu, [128, T], BF16)
        usbc = self.sb(stk_gu, [128, T], BF16)
        gur = self.sb(stk_gu, [1, 2 * T], F32)
        deq1 = self.scrow[0:1, 0:T]
        em("vector", lambda e: e.tensor_scalar(gur[0:1, 0:T], deq1,
                                               self.wsc[0:1, 0:1], None, OP.mult))
        v = em("vector", lambda e: e.tensor_scalar(gur[0:1, T:2 * T], deq1,
                                                   self.wsc[0:1, 1:2], None,
                                                   OP.mult), inc="v")
        Ss.barrier(("tensor", "vector"))
        self.bcast_row(gur[0:1, 0:T], gsbc, self.pb[2], T)
        self.bcast_row(gur[0:1, T:2 * T], usbc, self.pb[2], T)
        Ss.barrier()
        gap, uap = self.hbm["fgT"], self.hbm["fuT"]
        scA = self.hbm["scA"]
        stwg = Stream2(self, "sync", "wg")
        stwu = Stream2(self, "gpsimd", "wu")
        spl = Stream2(self, "scalar", "spl")
        for s2 in range(2):
            stwg.load(s2, wtg[s2][:], gap[0, :, s2 * 1024:(s2 + 1) * 1024])
            stwu.load(s2, wtu[s2][:], uap[0, :, s2 * 1024:(s2 + 1) * 1024])
        nld = [2, 2]
        qtok = [[None, None], [None, None]]
        mmtok = [[None, None], [None, None]]
        evg = [None, None]
        evu = [None, None]
        evh = [None, None]
        stg_free = [None, None]

        def quant_tile(m):
            for wi, (wtb, stwx, tqx, hbmw) in enumerate(
                    ((wtg, stwg, tqg, gap), (wtu, stwu, tqu, uap))):
                tq = tqx[m % 2]
                if mmtok[wi][m % 2] is not None:
                    wt("vector", *mmtok[wi][m % 2])
                q = None
                for hf in range(2):
                    h = 2 * m + hf
                    stwx.wait("vector", h % 2)
                    em("vector", lambda e, b=wtb[h % 2], w=wi: e.tensor_scalar(
                        sgn[:], b[:], self.abc3[:, 3 + w:4 + w], None,
                        OP.is_lt))
                    q = em("vector", lambda e, b=wtb[h % 2], hh=hf, w=wi, t2=tq:
                           e.scalar_tensor_tensor(
                               t2[:, hh * 1024:(hh + 1) * 1024], b[:],
                               self.abc3[:, w:w + 1], sgn[:], OP.is_gt,
                               OP.subtract), inc="v")
                    if nld[wi] < 128:
                        hn = nld[wi]
                        stwx.load(h % 2, wtb[h % 2][:],
                                  hbmw[hn // 2, :,
                                       (hn % 2) * 1024:(hn % 2 + 1) * 1024],
                                  guard=("v", q))
                        nld[wi] += 1
                qtok[wi][m % 2] = ("v", q)

        quant_tile(0)
        vstk = [None, None]
        for m in range(65):
            if m < 64:
                for wi, tqx in enumerate((tqg, tqu)):
                    tq = tqx[m % 2]
                    wt("tensor", *qtok[wi][m % 2])
                    ev = evg if wi == 0 else evu
                    for ni, (n0, nsz) in enumerate(NTS):
                        if ev[ni] is not None:
                            wt("tensor", *ev[ni])
                        p = self.pb[wi + 2 * ni]
                        for j in range(16):
                            em("tensor", lambda e, pp=p, t2=tq, jj=j, nn0=n0,
                               nns=nsz, st=(j == 0), sp=(j == 15):
                               e.matmul(pp[:, 0:nns],
                                        t2[:, jj * 128:(jj + 1) * 128],
                                        xq[:, jj * T + nn0:jj * T + nn0 + nns],
                                        start=st, stop=sp),
                               inc=("t" if j == 15 else None))
                    mmtok[wi][m % 2] = ("t", Ss.now("t"))
            if m >= 1:
                mp = m - 1
                a2 = None
                for ni, (n0, nsz) in enumerate(NTS):
                    wt("tensor", *vstk[ni])
                    if evh[ni] is not None:
                        wt("tensor", *evh[ni])
                    em("tensor", lambda e, p=self.pb[4 + ni], s=stg[mp % 2],
                       nn0=n0, nns=nsz:
                       e.matmul(p[:, 0:nns], self.h128pm[:, 0:128],
                                s[:, nn0:nn0 + nns], start=True, stop=True),
                       inc="t")
                    wt("scalar", "t", Ss.now("t"))
                    a2 = em("scalar", lambda e, p=self.pb[4 + ni],
                            s=stg[mp % 2], nn0=n0, nns=nsz:
                            e.activation(s[:, nn0:nn0 + nns], p[:, 0:nns],
                                         AF.Identity), inc="a")
                    evh[ni] = ("a", a2)
                spl.load(mp % 2,
                         scA.rearrange("(q mm) t -> mm q t", mm=64)[mp],
                         stg[mp % 2][:], guard=("a", a2))
            if m < 64:
                tdone = Ss.now("t")
                if m + 1 < 64:
                    quant_tile(m + 1)
                for ni, (n0, nsz) in enumerate(NTS):
                    wt("vector", "t", tdone)
                    if ni == 0 and m >= 2:
                        spl.wait("vector", m % 2)
                    evg[ni] = ("v", em(
                        "vector", lambda e, p=self.pb[0 + 2 * ni],
                        nn0=n0, nns=nsz:
                        e.tensor_tensor(utmp[:, 0:nns], p[:, 0:nns],
                                        gsbc[:, nn0:nn0 + nns], OP.mult),
                        inc="v"))
                    wt("scalar", "v", Ss.now("v"))
                    a = em("scalar", lambda e, nns=nsz: e.activation(
                        stile[:, 0:nns], utmp[:, 0:nns], AF.Silu), inc="a")
                    wt("vector", "a", a)
                    evu[ni] = ("v", em(
                        "vector", lambda e, p=self.pb[1 + 2 * ni], nns=nsz:
                        e.tensor_tensor(utmp[:, 0:nns], stile[:, 0:nns],
                                        p[:, 0:nns], OP.mult), inc="v"))
                    vstk[ni] = ("v", em(
                        "vector", lambda e, s=stg[m % 2], nn0=n0, nns=nsz:
                        e.tensor_tensor(s[:, nn0:nn0 + nns], utmp[:, 0:nns],
                                        usbc[:, nn0:nn0 + nns], OP.mult),
                        inc="v"))
        for e2 in ("vector", "scalar", "tensor", "sync", "gpsimd"):
            spl.wait(e2, 0)
            spl.wait(e2, 1)
        Ss.barrier()
        stk_gu.close()
        sXQ.close()
        z8 = self.sb(sF, [128, 64 * T], BF16, "z8")
        stk_mx = ExitStack()
        runmx = self.sb(stk_mx, [128, T], F32)
        runmn = self.sb(stk_mx, [128, T], F32)
        em("vector", lambda e: e.memset(runmx[:], 0.0))
        em("vector", lambda e: e.memset(runmn[:], 0.0), inc="v")
        Ss.barrier(("tensor", "vector"))
        rel = Stream2(self, "sync", "rel", nslots=8)
        for k in range(8):
            rel.load(k, z8[:, k * T:(k + 1) * T],
                     scA[k * 128:(k + 1) * 128, :])
        ps_rel = [None, None]
        it = 0
        for k in range(64):
            rel.wait("tensor", k % 8)
            for ni, (n0, nsz) in enumerate(NTS):
                pi = it % 2
                ps = self.pb[pi]
                if ps_rel[pi] is not None:
                    wt("tensor", *ps_rel[pi])
                for par in range(2):
                    em("tensor", lambda e, p=ps, kk=k, pp=par, nn0=n0, nns=nsz:
                       e.matmul(p[pp * 64:pp * 64 + 64, 0:nns],
                                self.h64d[:, pp * 64:pp * 64 + 64],
                                z8[:, kk * T + nn0:kk * T + nn0 + nns],
                                start=True, stop=True),
                       inc=("t" if par == 1 else None))
                wt("vector", "t", Ss.now("t"))
                em("vector", lambda e, p=ps, nn0=n0, nns=nsz:
                   e.tensor_tensor(runmx[:, nn0:nn0 + nns],
                                   runmx[:, nn0:nn0 + nns],
                                   p[:, 0:nns], OP.max))
                cnt = em("vector", lambda e, p=ps, nn0=n0, nns=nsz:
                         e.tensor_tensor(runmn[:, nn0:nn0 + nns],
                                         runmn[:, nn0:nn0 + nns],
                                         p[:, 0:nns], OP.min), inc="v")
                ps_rel[pi] = ("v", cnt)
                it += 1
            if k + 8 < 64:
                rel.load((k + 8) % 8, z8[:, (k + 8) * T:(k + 9) * T],
                         scA[(k + 8) * 128:(k + 9) * 128, :],
                         guard=("t", Ss.now("t")))
        Ss.barrier()
        mrow = self.sb(stk_mx, [1, 3 * T], F32)
        self.token_max_row(runmx, runmn, mrow[0:1, 0:T],
                           mrow[0:1, 2 * T:3 * T], stk_mx)
        Ss.barrier(("vector",))
        rn = float(np.float32(np.sqrt(8192.0)))
        e2 = self.scrow[0:1, T:2 * T]
        sc2 = mrow[0:1, T:2 * T]
        tmpr = mrow[0:1, 2 * T:3 * T]
        em("vector", lambda e: e.tensor_scalar(e2, mrow[0:1, 0:T],
                                               self.c_invrn2, EPS,
                                               OP.mult, OP.max))
        em("vector", lambda e: e.tensor_single_scalar(tmpr, e2, rn, OP.mult))
        em("vector", lambda e: e.reciprocal(sc2, tmpr))
        em("vector", lambda e: e.tensor_single_scalar(sc2, sc2, 127.0, OP.mult))
        em("vector", lambda e: e.tensor_single_scalar(e2, e2, 1.0 / 127.0,
                                                      OP.mult))
        v = em("vector", lambda e: e.tensor_scalar(e2, e2,
                                                   self.wsc[0:1, 2:3], None,
                                                   OP.mult), inc="v")
        Ss.barrier()
        self.scbc2 = self.sb(self.stk, [128, T], F32, hi=True)
        self.ebc2 = self.sb(self.stk, [128, T], F32, hi=True)
        self.bcast_row(sc2, self.scbc2, self.pb[2], T)
        self.bcast_row(e2, self.ebc2, self.pb[2], T)
        Ss.barrier()
        stk_mx.close()
        return z8

    def p7(self, stk, xq2):
        Ss = self.S
        em, wt, dma = Ss.emit, Ss.wait, Ss.dma
        z8 = xq2
        scbc = self.scbc2
        lv0 = [self.sb(stk, [128, 512], F32) for _ in range(2)]
        lwt = [self.sb(stk, [128, 512], F32) for _ in range(2)]
        lct = [self.sb(stk, [128, 512], F32) for _ in range(2)]
        stkL = ExitStack()
        lx = [tuple(self.sb(stkL, [128, 512], F32) for _ in range(3))
              for _ in range(2)]
        wt_ = [self.sb(stk, [128, 1024], F32) for _ in range(4)]
        td = self.sb(stk, [128, 8192], BF16)
        sgn = self.sb(stk, [128, 1024], BF16)
        fdap = self.hbm["fdT"]
        h2ap = self.hbm["sch2"]
        st2f = Stream2(self, "scalar", "fd7", nslots=4)
        st2h = Stream2(self, "sync", "h27")
        st2o = Stream2(self, "sync", "o7")

        def fd_src(c):
            cc = c % 128
            return fdap[cc // 8, :, (cc % 8) * 1024:(cc % 8 + 1) * 1024]

        for c4 in range(4):
            st2f.load(c4, wt_[c4][:], fd_src(c4))
        qv = {}

        def fdq(p, m, tok_fn):
            toks = []
            for q8 in range(8):
                c = p * 128 + m * 8 + q8
                st2f.wait("vector", c % 4)
                if tok_fn is not None:
                    wt("vector", *tok_fn(q8))
                em("vector", lambda e, b=wt_[c % 4]: e.tensor_scalar(
                    sgn[:], b[:], self.abc3[:, 5:6], None, OP.is_lt))
                q = em("vector", lambda e, b=wt_[c % 4],
                       d=slice(q8 * 1024, (q8 + 1) * 1024):
                       e.scalar_tensor_tensor(td[:, d], b[:],
                                              self.abc3[:, 2:3], sgn[:],
                                              OP.is_gt, OP.subtract), inc="v")
                toks.append(q)
                if c + 4 < 256:
                    st2f.load((c + 4) % 4, wt_[(c + 4) % 4][:],
                              fd_src(c + 4), guard=("v", q))
            qv[(p, m)] = toks

        psfree = [None, None]
        pf = [None, None]

        def h64_pair(kA, n0):
            ptoks = []
            for k in (kA, kA + 1):
                pi = k % 2
                if psfree[pi] is not None:
                    wt("tensor", *psfree[pi])
                for par in range(2):
                    em("tensor", lambda e, p=self.pb[pi], kk=k, pp=par, nn0=n0:
                       e.matmul(p[pp * 64:pp * 64 + 64, 0:512],
                                self.h64d[:, pp * 64:pp * 64 + 64],
                                z8[:, kk * T + nn0:kk * T + nn0 + 512],
                                start=True, stop=True),
                       inc=("t" if par == 1 else None))
                ptoks.append(("t", Ss.now("t")))
            first, fin = self.ladder2_pair(
                [self.pb[k % 2][:, 0:512] for k in (kA, kA + 1)],
                [scbc[:, n0:n0 + 512]] * 2,
                [z8[:, k * T + n0:k * T + n0 + 512] for k in (kA, kA + 1)],
                [(lv0[k % 2], lwt[k % 2], lct[k % 2]) for k in (kA, kA + 1)],
                ptoks)
            for idx, k in enumerate((kA, kA + 1)):
                psfree[k % 2] = ("v", first[idx])
            return fin

        fdq(0, 0, None)
        psfree4 = [None] * 4
        bufs4 = [(lv0[0], lwt[0], lct[0]), (lv0[1], lwt[1], lct[1]),
                 lx[0], lx[1]]
        for kA in range(0, 64, 4):
            ptoks = []
            for k in range(kA, kA + 4):
                pi = k % 4
                if psfree4[pi] is not None:
                    wt("tensor", *psfree4[pi])
                for par in range(2):
                    em("tensor", lambda e, p=self.pb[pi], kk=k, pp=par:
                       e.matmul(p[pp * 64:pp * 64 + 64, 0:512],
                                self.h64d[:, pp * 64:pp * 64 + 64],
                                z8[:, kk * T:kk * T + 512],
                                start=True, stop=True),
                       inc=("t" if par == 1 else None))
                ptoks.append(("t", Ss.now("t")))
            first4, fin4 = self.ladder2_pair(
                [self.pb[k % 4][:, 0:512] for k in range(kA, kA + 4)],
                [scbc[:, 0:512]] * 4,
                [z8[:, k * T:k * T + 512] for k in range(kA, kA + 4)],
                bufs4, ptoks)
            for idx, k in enumerate(range(kA, kA + 4)):
                psfree4[k % 4] = ("v", first4[idx])
        ladfin = [Ss.now("v"), None]
        psfree[0] = psfree4[0]
        psfree[1] = psfree4[1]
        for e2 in ("vector", "scalar", "gpsimd", "sync"):
            Ss.wait(e2, "v", ladfin[0])
        stkL.close()
        h2t = [self.sb(stk, [128, 512], F32) for _ in range(2)]
        ot = [self.sb(stk, [128, 512], F32) for _ in range(2)]
        tmpf = self.sb(stk, [128, 512], F32)
        st2h.load(0, h2t[0][:], h2ap[0:128, 0:512])
        st2h.load(1, h2t[1][:], h2ap[128:256, 0:512])

        jt = {}
        for p, (n0, nsz) in enumerate(NTS):
            if p == 1:
                fdq(1, 0, lambda q8: ("t", jt[(0, 15, q8)]))
            for m in range(16):
                hs = p * 16 + m
                if m == 0:
                    wt("tensor", "v", ladfin[p])
                pb4 = self.pb[4 + m % 2]
                if pf[m % 2] is not None:
                    wt("tensor", *pf[m % 2])
                for j in range(64):
                    if j % 8 == 0:
                        wt("tensor", "v", qv[(p, m)][j // 8])
                    em("tensor", lambda e, pp=pb4, jj=j, nn0=n0:
                       e.matmul(pp[:, 0:512], td[:, jj * 128:(jj + 1) * 128],
                                z8[:, jj * T + nn0:jj * T + nn0 + 512],
                                start=(jj == 0), stop=(jj == 63)),
                       inc=("t" if j % 8 == 7 else None))
                    if j % 8 == 7:
                        jt[(p, m, j // 8)] = Ss.now("t")
                mmdone = Ss.now("t")
                if p == 0 and m + 1 < 16:
                    fdq(0, m + 1, lambda q8, mm=m: ("t", jt[(0, mm, q8)]))
                elif p == 1 and m + 1 < 16:
                    fdq(1, m + 1, lambda q8, mm=m: ("t", jt[(1, mm, q8)]))
                if p == 0:
                    h64_pair(4 * m, 512)
                    h64_pair(4 * m + 2, 512)
                    if m == 15:
                        ladfin[1] = Ss.now("v")
                st2h.wait("vector", hs % 2)
                if hs >= 2:
                    st2o.wait("vector", hs % 2)
                wt("vector", "t", mmdone)
                tk = em("vector", lambda e, pp=pb4, nn0=n0: e.tensor_tensor(
                    tmpf[:], pp[:, 0:512], self.ebc2[:, nn0:nn0 + 512],
                    OP.mult), inc="v")
                pf[m % 2] = ("v", tk)
                cnt = em("vector", lambda e, oo=ot[hs % 2], hh2=h2t[hs % 2]:
                         e.tensor_tensor(oo[:], tmpf[:], hh2[:], OP.add),
                         inc="v")
                wt("sync", "v", cnt)
                st2o.load(hs % 2,
                          self.hbm["outT"][m * 128:(m + 1) * 128,
                                           n0:n0 + 512], ot[hs % 2][:])
                if hs + 2 < 32:
                    p2, m2 = divmod(hs + 2, 16)
                    n02 = NTS[p2][0]
                    st2h.load(hs % 2, h2t[hs % 2][:],
                              h2ap[m2 * 128:(m2 + 1) * 128, n02:n02 + 512],
                              guard=("v", cnt))
        st2o.wait("sync", 0)
        st2o.wait("sync", 1)
        Ss.barrier()

    def _phase_gate(self, n):
        if n >= PHASE_LIMIT:
            Ss = self.S
            Ss.barrier()
            Ss.dma("sync", self.hbm["outT"][0:128, :],
                   self.identf[:, 0:1] if False else self.arena.t[0:128, 0:2048].bitcast(F32)[:, 0:1024], "st")
            Ss.wait("sync", "st", None)
            Ss.barrier()
            return True
        return False

    def emit_all(self):
        Ss = self.S
        sP0 = ExitStack()
        self.p0(sP0)
        sP0.close()
        Ss.barrier()
        if self._phase_gate(1):
            return
        sB = ExitStack()
        self.siluz = self.sb(sB, [128, 16 * T], BF16, "siluz")
        sROWS = ExitStack()
        self.rowsA = self.sb(sROWS, [32, 4 * T], F32, "rowsA")
        self.Hn = self.sb(sROWS, [64, 2048], F32, "Hn")
        self.Hn2b = self.sb(sROWS, [128, 2048], BF16, "Hn2b")
        sSTAT = ExitStack()
        Ss.emit("vector", lambda e: e.memset(self.Hn[:], 0.0), inc="v")
        Ss.barrier()
        sA = ExitStack()
        self.xbc = self.sb(sA, [128, 24 * T], BF16, "xbc")
        sR = ExitStack()
        self.rm1 = self.sb(sR, [128, 16 * TN], BF16, "rm1")
        self.rowsB = self.sb(sR, [32, 4 * T], F32, "rowsB")
        with ExitStack() as sp:
            statp = self.sb(sp, [1, 3 * TN], F32)
            self.stage1("xprevT", sp, full=False, stat=statp)
        with ExitStack() as s:
            self.scan(s, state_only=True)
        Ss.barrier()
        if self._phase_gate(3):
            return
        self.stat_main = self.sb(sSTAT, [1, 3 * TN], F32, "statm", hi=True)
        self.stage1("xT", sA, full=True, stat=self.stat_main)
        sR.close()
        if self._phase_gate(4):
            return
        self.ya = self.sb(sB, [128, 16 * T], BF16, "ya")
        with ExitStack() as s:
            self.scan(s, state_only=False)
        Ss.barrier()
        sA.close()
        sROWS.close()
        if self._phase_gate(5):
            return
        sC = ExitStack()
        self.r2a = self.sb(sC, [128, 16 * T], F32, "r2a", hi=True)
        with ExitStack() as s:
            self.p3(s)
        sB.close()
        sSTAT.close()
        Ss.barrier()
        if self._phase_gate(6):
            return
        sD = ExitStack()
        self.rms2 = self.sb(sD, [128, 16 * T], BF16, "rms2", hi=True)
        with ExitStack() as s:
            self.p4(s)
        sD.close()
        Ss.barrier()
        if self._phase_gate(7):
            return
        sXQ = ExitStack()
        with ExitStack() as s:
            self.p5(s, sC, sXQ)
        Ss.barrier()
        if self._phase_gate(8):
            return
        sF = ExitStack()
        with ExitStack() as s:
            z8 = self.p6(s, sF, sXQ)
        Ss.barrier()
        self.sSCROW.close()
        if self._phase_gate(9):
            return
        with ExitStack() as s:
            self.p7(s, z8)
        sF.close()

    def finalize(self):
        with self.nc.Block() as block:
            self.S.build(block)
        self.stk.close()
        return self.nc


_HOST_CACHE = {}


def _tile_w(Wo):
    M, K = Wo.shape
    mb, kb = M // 128, K // 128
    return np.ascontiguousarray(
        Wo.reshape(mb, 128, kb, 128).transpose(0, 3, 2, 1).reshape(mb, 128, K),
        dtype=np.float32)


def _host_prepare(inputs):
    f32 = np.float32
    win = np.asarray(inputs["L0_attn_in_proj_w"], f32)
    winp = np.zeros((41 * 128, 2048), f32)
    winp[0:5152] = win
    consts = {
        "winT": _tile_w(winp),
        "woutT": _tile_w(np.asarray(inputs["L0_attn_out_proj_w"], f32)),
        "wgT": _tile_w(np.asarray(inputs["L0_attn_gate_w"], f32)),
        "wuT": _tile_w(np.asarray(inputs["L0_attn_up_w"], f32)),
        "wdT": _tile_w(np.asarray(inputs["L0_attn_down_w"], f32)),
        "fgT": _tile_w(np.asarray(inputs["L0_ffn_w_gate"], f32)),
        "fuT": _tile_w(np.asarray(inputs["L0_ffn_w_up"], f32)),
    }
    fd = np.asarray(inputs["L0_ffn_w_down"], f32)
    kidx = np.empty(8192, np.int64)
    for j in range(64):
        for kp in range(128):
            kidx[j * 128 + kp] = (kp % 64) * 128 + 2 * j + kp // 64
    consts["fdT"] = _tile_w(np.ascontiguousarray(fd[:, kidx]))
    consts["convw"] = np.ascontiguousarray(
        np.asarray(inputs["L0_attn_conv_w"], f32).reshape(24, 128, 4)
        .transpose(1, 0, 2).reshape(128, 96))
    consts["convb"] = np.ascontiguousarray(
        np.asarray(inputs["L0_attn_conv_b"], f32).reshape(24, 128).T)
    consts["dtb"] = np.asarray(inputs["L0_attn_dt_bias"], f32).reshape(32, 1)
    consts["alog"] = np.asarray(inputs["L0_attn_A_log"], f32).reshape(32, 1)
    for nm, key in (("wln", "L0_attn_input_ln_w"), ("wpost", "L0_attn_post_ln_w"),
                    ("ssmw", "L0_attn_ssm_norm_w")):
        consts[nm] = np.ascontiguousarray(
            np.asarray(inputs[key], f32).reshape(16, 128).T)
    Dh = np.asarray(inputs["L0_attn_D"], f32)
    dd = np.zeros((128, 2048), f32)
    for h in range(32):
        g = h // 4
        b = (g % 2) * 64
        for p in range(64):
            dd[b + p, h * 64 + p] = Dh[h]
    consts["ddiag"] = dd
    H128 = _hadamard(128)
    consts["h128pm"] = np.ascontiguousarray(np.hstack([H128, -H128]))
    H64 = _hadamard(64)
    h64z = np.zeros((128, 128), np.float32)
    h64z[0:64, 0:64] = H64
    h64z[64:128, 64:128] = H64
    consts["h64d"] = h64z
    tri = (np.arange(Q)[:, None] <= np.arange(Q)[None, :]).astype(f32)
    consts["tri4"] = np.ascontiguousarray(np.tile(tri, (1, 4)))
    consts["identf"] = np.eye(128, dtype=f32)
    rowc = np.zeros((1, 448), f32)
    rowc[0, 0:128] = 1.0
    rowc[0, 128:192] = 1.0
    rowc[0, 320:384] = 1.0
    rowc[0, 384] = np.float32(1.0) / np.float32(np.sqrt(2048.0))
    rowc[0, 385] = np.float32(1.0) / np.float32(np.sqrt(8192.0))
    rowc[0, 386] = np.float32(1.0 / (8192.0 * 2048.0))
    consts["rowc"] = rowc
    consts["onescol"] = np.ones((128, 1), f32)
    lt = np.zeros((128, 24), f32)
    for j, kk in enumerate(LAD_KS):
        lt[:, j] = np.float32(kk) - np.float32(128.5)
        lt[:, 8 + j] = LAD_THR[j]
        lt[:, 16 + j] = -LAD_THR[j]
    consts["ladthr"] = lt
    return consts


def kernel(**inputs):
    consts = _host_prepare(inputs)
    x = np.asarray(inputs["L0_attn_hidden_states"], np.float32)
    in_maps = []
    for core in range(8):
        b, half = core // 2, core % 2
        t0 = half * T
        xs = np.ascontiguousarray(x[b].T)
        xT = np.zeros((D, TN), np.float32)
        xT[:, HALO:] = xs[:, t0:t0 + T]
        if half == 1:
            xT[:, :HALO] = xs[:, t0 - HALO:t0]
        xprevT = np.zeros((D, TN), np.float32)
        if half == 1:
            xprevT[:, HALO:] = xs[:, 0:T]
        m = dict(consts)
        m["xT"] = xT
        m["xprevT"] = xprevT
        in_maps.append(m)
    P = Prog()
    P.emit_all()
    nc = P.finalize()
    res = run_bass_kernel_spmd(nc, in_maps, core_ids=list(range(8)))
    out = np.empty((4, 2048, 2048), np.float32)
    for core in range(8):
        b, half = core // 2, core % 2
        t0 = half * T
        out[b, t0:t0 + T, :] = res.results[core]["outT"].T
    return out
